# version 05
# baseline (speedup 1.0000x reference)
import contextlib
import numpy as np
import ml_dtypes
import concourse.bass as bass
import concourse.mybir as mybir
from concourse.bass_utils import run_bass_kernel_spmd
from concourse.alu_op_type import AluOpType as OP

F32 = mybir.dt.float32
BF16 = mybir.dt.bfloat16
AF = mybir.ActivationFunctionType
AXX = mybir.AxisListType.X
ENGINES = ("tensor", "vector", "scalar", "gpsimd", "sync")
PI = float(np.pi)

CFG_FULL = dict(S=2048, D=1024, NST=16, F=4096, E=8, BATCHES=2)
ROUTER_LITE = [0]
ALL_PARTS = ("attn_a", "ffn1", "attn_b", "ffn2")


class K:
    def __init__(self, nc):
        self.nc = nc
        self.sems = {}
        self.shadow = {}
        self.stack = contextlib.ExitStack()
        for name in ("pe", "dve", "act", "pool", "dsync", "dgp", "dact",
                     "dgp0", "dgp1", "dwf", "dwd"):
            self.sems[name] = self.stack.enter_context(nc.semaphore(name))
            self.shadow[name] = 0
        self.compute_sems = ("pe", "dve", "act", "pool")

    def close(self):
        self.stack.close()


class Ph:

    def __init__(self, k, name=""):
        self.k = k
        self.name = name
        self.ops = {e: [] for e in ENGINES}
        self.base = {s: k.shadow[s] for s in k.compute_sems}

    def op(self, eng, fn, waits=(), inc=None, n=1):
        tok = None
        if inc is not None:
            self.k.shadow[inc] += n
            tok = (inc, self.k.shadow[inc])
        wl = []
        for w in waits:
            if w is None:
                continue
            if isinstance(w, list):
                wl.extend(x for x in w if x is not None)
            else:
                wl.append(w)
        self.ops[eng].append((fn, wl, inc, n))
        return tok

    def dma(self, eng, dst, src, waits=(), sem=None):
        if sem is None:
            sem = {"sync": "dsync", "gpsimd": "dgp", "scalar": "dact"}[eng]
        return self.op(eng, lambda e, d=dst, s=src: e.dma_start(d, s),
                       waits=waits, inc=sem, n=16)

    def fence(self):
        return [("dve", self.k.shadow["dve"]), ("act", self.k.shadow["act"])]

    def emit(self):
        k = self.k
        with k.nc.Block(self.name or None) as block:
            for eng in ENGINES:
                if not self.ops[eng]:
                    continue

                def body(e, ops=self.ops[eng], base=dict(self.base)):
                    hw = base
                    for fn, waits, inc, n in ops:
                        agg = {}
                        for (sem, val) in waits:
                            if val > agg.get(sem, 0):
                                agg[sem] = val
                        for sem, val in agg.items():
                            if hw.get(sem, 0) >= val:
                                continue
                            e.wait_ge(k.sems[sem], val)
                            hw[sem] = val
                        instr = fn(e)
                        if inc is not None:
                            instr.then_inc(k.sems[inc], n)

                getattr(block, eng)(body)


def build(cfg, enable=ALL_PARTS):
    S, D, NST, F, E, NB = (cfg["S"], cfg["D"], cfg["NST"], cfg["F"],
                           cfg["E"], cfg["BATCHES"])
    TT, DT, TC, HC = S // 128, D // 128, S // 512, D // 512
    NW = F // 512
    NHP = D // 128
    TKT = TT
    assert S % 512 == 0 and D % 512 == 0 and F % 512 == 0 and DT >= 4

    nc = bass.Bass("TRN2", target_bir_lowering=False)
    k = K(nc)

    def din(name, shape, dt=F32):
        return nc.dram_tensor(name, list(shape), dt, kind="ExternalInput")

    x_d = din("x", (NB, S, D))
    out_d = nc.dram_tensor("out", [NB, S, D], F32, kind="ExternalOutput")
    guT_d = din("guT", (E, D, 2 * F))
    dwT_d = din("dwT", (E, F, D))
    gwT_d = din("gwT", (D, E))
    woTa_d = din("woTa", (D, D))
    wbT_d = din("wbT", (D, NST))
    wtT_d = din("wtT", (DT, 128))
    wcT_d = din("wcT", (NST, D))
    A_d = din("A", (NST,))
    bb_d = din("bb", (NST,))
    bt_d = din("bt", (1,))
    Dg_d = din("Dg", (DT, 128))
    wz_d = din("wz", (D, D))
    wq_d = din("wq", (D, D))
    wr_d = din("wr", (D, D))
    wh_d = din("wh", (D, D))
    wob_d = din("wob", (D, D))
    bq_d = din("bq", (DT, 128))
    br_d = din("br", (DT, 128))
    wphi_d = din("wphi", (64, 64))
    bphi_d = din("bphi", (64,))
    id32_d = din("id32", (128, 128))
    idbf_d = din("idbf", (128, 128), BF16)
    ones_d = din("ones_f", (1, 128))
    onesbf_d = din("ones_bf", (1, 512), BF16)
    ones16_d = din("ones16", (1, 16))
    brow_d = {nm: din("b_" + nm, (D,))
              for nm in ("bc", "boa", "bz", "bh", "bob", "lng", "lnb")}

    st = contextlib.ExitStack()
    sb = lambda name, p, f, dt: st.enter_context(nc.sbuf_tensor(name, [p, f], dt))
    _uid = [0]

    def lb(pst, name, p, f, dt):
        _uid[0] += 1
        return pst.enter_context(nc.sbuf_tensor(f"{name}_{_uid[0]}", [p, f], dt))

    x_sb = sb("x_sb", 128, TT * D, F32)
    xnT = sb("xnT", 128, DT * S, BF16)
    big = sb("big", 128, TT * D, BF16)
    wfull = sb("wfull", 128, DT * D, BF16)
    stats = sb("stats", 128, 4 * TT, F32)
    zstats = sb("zstats", 128, 4 * TT, F32)
    rsc = sb("rsc", 128, 40, F32)
    comb = sb("comb", 128, TT * 8, F32)
    id32 = sb("id32s", 128, 128, F32)
    idbf = sb("idbfs", 128, 128, BF16)
    onesf = sb("onesf", 1, 128, F32)
    onesb = sb("onesb", 1, 512, BF16)
    ones16 = sb("ones16s", 1, 16, F32)
    onescol = sb("onescol", 128, 1, BF16)
    eps5 = sb("eps5", 128, 1, F32)
    pihalf = sb("pihalf", 128, 1, F32)
    Acol = sb("Acol", NST, 1, F32)
    bbcol = sb("bbcol", NST, 1, F32)
    btcol = sb("btcol", 1, 1, F32)
    bphic = sb("bphic", 64, 1, F32)
    Dcol = sb("Dcol", 128, DT, F32)
    bqcol = sb("bqcol", 128, DT, F32)
    brcol = sb("brcol", 128, DT, F32)
    rows8 = sb("rows8", DT, 128, F32)
    wbT = sb("wbT_s", 128, DT * NST, BF16)
    wtT = sb("wtT_s", 128, DT, BF16)
    wcT = sb("wcT_s", NST, D, BF16)
    gwT = sb("gwT_s", 128, DT * E, F32)
    wphi = sb("wphi_s", 128, 64, BF16)
    brow = {nm: sb(f"brow_{nm}", 1, D, BF16)
            for nm in ("bc", "boa", "bz", "bh", "bob")}

    phases = []

    def new_ph(name):
        ph = Ph(k, name)
        phases.append(ph)
        return ph

    def apS(t, off, dims):
        return bass.AP(t, off, dims)

    def xsl(t, lo=0, wdt=None):
        wdt = D if wdt is None else wdt
        return x_sb[:, t * D + lo: t * D + lo + wdt]

    def xnsl(j, lo, wdt):
        return xnT[:, j * S + lo: j * S + lo + wdt]

    def bigsl(lo, wdt):
        return big[:, lo:lo + wdt]

    def mmgroup(ph, out_ap, pieces, waits=(), extra=()):
        allp = list(pieces) + list(extra)

        def fn(e, allp=allp, out_ap=out_ap):
            n = len(allp)
            last = None
            for i, (l, r) in enumerate(allp):
                last = e.matmul(out_ap, l, r, start=(i == 0), stop=(i == n - 1))
            return last

        return ph.op("tensor", fn, waits=waits, inc="pe")

    ph = new_ph("setup")
    td = []
    td.append(ph.dma("sync", id32[:, :], id32_d.ap()))
    td.append(ph.dma("sync", idbf[:, :], idbf_d.ap()))
    td.append(ph.dma("sync", onesf[:, :], ones_d.ap()))
    td.append(ph.dma("sync", onesb[:, :], onesbf_d.ap()))
    td.append(ph.dma("sync", ones16[:, :], ones16_d.ap()))
    td.append(ph.dma("sync", Acol[:, 0:1], apS(A_d, 0, [[1, NST], [1, 1]])))
    td.append(ph.dma("sync", bbcol[:, 0:1], apS(bb_d, 0, [[1, NST], [1, 1]])))
    td.append(ph.dma("sync", btcol[:, 0:1], apS(bt_d, 0, [[1, 1], [1, 1]])))
    td.append(ph.dma("sync", bphic[:, 0:1], apS(bphi_d, 0, [[1, 64], [1, 1]])))
    for nm in brow:
        td.append(ph.dma("gpsimd", brow[nm][:, :],
                         apS(brow_d[nm], 0, [[D, 1], [1, D]])))
    td.append(ph.dma("sync", gwT[:, :],
                     apS(gwT_d, 0, [[E, 128], [128 * E, DT], [1, E]])))
    td.append(ph.dma("gpsimd", wbT[:, :],
                     apS(wbT_d, 0, [[NST, 128], [128 * NST, DT], [1, NST]])))
    td.append(ph.dma("gpsimd", wphi[0:64, :], wphi_d.ap()))
    td.append(ph.dma("gpsimd", wphi[64:128, :], wphi_d.ap()))
    td.append(ph.dma("gpsimd", wcT[:, :], wcT_d.ap()))
    ph.op("vector", lambda e: e.memset(eps5[:, 0:1], 1e-5), inc="dve")
    ph.op("vector", lambda e: e.memset(pihalf[:, 0:1], PI / 2), inc="dve")
    ph.op("vector", lambda e: e.memset(onescol[:, 0:1], 1.0), inc="dve")
    tprev = None
    setup_ps = contextlib.ExitStack()
    ps_c = setup_ps.enter_context(nc.psum_tensor("ps_c", [128, DT], F32))
    trd = None
    for dd, dst in ((Dg_d, Dcol), (bq_d, bqcol), (br_d, brcol), (wtT_d, wtT)):
        tl = ph.dma("scalar", rows8[0:DT, :], dd.ap(), waits=[trd])
        tt = ph.op("tensor",
                   lambda e: e.transpose(ps_c[:, :], rows8[0:DT, :], id32[0:DT, 0:DT]),
                   waits=[tl, trd] + td, inc="pe")
        trd = ph.op("vector",
                    lambda e, dst=dst: e.tensor_copy(dst[:, :], ps_c[:, :]),
                    waits=[tt], inc="dve")
    setup_ps.close()

    def ln_phase(ph, pst, router, wait_tokens=()):
        xn32 = [lb(pst, "xn32", 128, D, F32) for i in range(2)]
        txn = [lb(pst, "txn", 128, D, F32) for i in range(2)] if router else None
        sqdump = lb(pst, "sqdump", 128, D, F32)
        psT = [pst.enter_context(nc.psum_tensor(f"psT{i}_{ph.name}", [128, D], F32))
               for i in range(2)]
        psr = ([pst.enter_context(nc.psum_tensor(f"psr{i}_{ph.name}", [128, 8], F32))
                for i in range(2)] if router else None)
        tsum = tsq = None
        for t in range(TT):
            tsum = ph.op("vector",
                         lambda e, t=t: e.reduce_sum(stats[:, t:t + 1], xsl(t), axis=AXX),
                         waits=list(wait_tokens),
                         inc="dve" if t == TT - 1 else None)
            tsq = ph.op("scalar",
                        lambda e, t=t: e.activation(
                            sqdump[:, :], xsl(t), AF.Square,
                            accum_out=stats[:, TT + t:TT + t + 1]),
                        waits=list(wait_tokens) + [tsq], inc="act")
        t1 = ph.op("vector",
                   lambda e: e.tensor_scalar_mul(stats[:, 2 * TT:3 * TT],
                                                 stats[:, 0:TT], 1.0 / D),
                   waits=[tsum], inc="dve")
        t2 = ph.op("vector",
                   lambda e: e.tensor_tensor(stats[:, 3 * TT:4 * TT],
                                             stats[:, 2 * TT:3 * TT],
                                             stats[:, 2 * TT:3 * TT], OP.mult),
                   waits=[t1], inc="dve")
        t3 = ph.op("vector",
                   lambda e: e.scalar_tensor_tensor(stats[:, 3 * TT:4 * TT],
                                                    stats[:, TT:2 * TT], 1.0 / D,
                                                    stats[:, 3 * TT:4 * TT],
                                                    OP.mult, OP.subtract),
                   waits=[t2, tsq], inc="dve")

        t4 = ph.op("scalar",
                   lambda e: e.activation(stats[:, 3 * TT:4 * TT],
                                          stats[:, 3 * TT:4 * TT],
                                          AF.Sqrt, bias=eps5[:, 0:1]),
                   waits=[t3], inc="act")
        t5 = ph.op("vector",
                   lambda e: e.reciprocal(stats[:, 3 * TT:4 * TT],
                                          stats[:, 3 * TT:4 * TT]),
                   waits=[t4], inc="dve")
        ev_toks = [None, None]
        tr_toks = [None, None]
        cp_toks = [None, None]
        rt_prev = None
        last_xn = None
        for t in range(TT):
            b = t % 2
            tts = ph.op("vector",
                        lambda e, t=t, b=b: e.tensor_scalar(
                            xn32[b][:, :], xsl(t),
                            stats[:, 2 * TT + t:2 * TT + t + 1],
                            stats[:, 3 * TT + t:3 * TT + t + 1],
                            OP.subtract, OP.mult),
                        waits=[t5, tr_toks[b]], inc="dve")

            def trf(e, b=b):
                last = None
                for j in range(DT):
                    last = e.transpose(psT[b][:, j * 128:(j + 1) * 128],
                                       xn32[b][:, j * 128:(j + 1) * 128],
                                       id32[:, :])
                return last

            ttr = ph.op("tensor", trf,
                        waits=[tts, ev_toks[b], cp_toks[b]], inc="pe")
            tr_toks[b] = ttr
            ev_toks[b] = ph.op("vector",
                               lambda e, t=t, b=b: e.tensor_copy(
                                   apS(xnT, t * 128,
                                       [[DT * S, 128], [S, DT], [1, 128]]),
                                   apS(psT[b], 0,
                                       [[D, 128], [128, DT], [1, 128]])),
                               waits=[ttr], inc="dve")
            last_xn = ev_toks[b]
            if router:
                cp_toks[b] = ph.op("scalar",
                                   lambda e, b=b: e.copy(txn[b][:, :], psT[b][:, :]),
                                   waits=[ttr, ev_toks[b]], inc="act")
                if ROUTER_LITE[0] >= 2:
                    rt_prev = ph.op("vector",
                                    lambda e, t=t: e.memset(comb[:, t * 8:t * 8 + E], 0.125),
                                    waits=[rt_prev], inc="dve")
                    continue
                trr = mmgroup(ph, psr[b][:, 0:E],
                              [(txn[b][:, j * 128:(j + 1) * 128],
                                gwT[:, j * E:(j + 1) * E]) for j in range(DT)],
                              waits=[cp_toks[b], rt_prev])
                c0 = t * 8
                if ROUTER_LITE[0] == 1:
                    rt_prev = ph.op("vector",
                                    lambda e, b=b, c0=c0: e.tensor_copy(
                                        comb[:, c0:c0 + E], psr[b][:, 0:E]),
                                    waits=[trr, rt_prev], inc="dve")
                    continue
                a1 = ph.op("vector",
                           lambda e, b=b: e.reduce_max(rsc[:, 0:1], psr[b][:, 0:E], axis=AXX),
                           waits=[trr, rt_prev], inc="dve")
                a2 = ph.op("vector",
                           lambda e, b=b: e.tensor_scalar(rsc[:, 12:12 + E], psr[b][:, 0:E],
                                                          rsc[:, 0:1], None, OP.is_ge),
                           waits=[a1], inc="dve")
                a3 = ph.op("vector",
                           lambda e, b=b: e.scalar_tensor_tensor(rsc[:, 4:4 + E],
                                                                 rsc[:, 12:12 + E], -1e30,
                                                                 psr[b][:, 0:E],
                                                                 OP.mult, OP.add),
                           waits=[a2], inc="dve")
                a4 = ph.op("vector",
                           lambda e: e.reduce_max(rsc[:, 1:2], rsc[:, 4:4 + E], axis=AXX),
                           waits=[a3], inc="dve")
                a5 = ph.op("vector",
                           lambda e: e.tensor_scalar(rsc[:, 20:20 + E], rsc[:, 4:4 + E],
                                                     rsc[:, 1:2], None, OP.is_ge),
                           waits=[a4], inc="dve")
                a6 = ph.op("vector",
                           lambda e: e.tensor_tensor(rsc[:, 2:3], rsc[:, 1:2],
                                                     rsc[:, 0:1], OP.subtract),
                           waits=[a5], inc="dve")
                a7 = ph.op("scalar",
                           lambda e: e.activation(rsc[:, 3:4], rsc[:, 2:3], AF.Exp),
                           waits=[a6], inc="act")
                a8 = ph.op("vector",
                           lambda e: e.tensor_scalar(rsc[:, 2:3], rsc[:, 3:4],
                                                     1.0, None, OP.add),
                           waits=[a7], inc="dve")
                a9 = ph.op("vector",
                           lambda e: e.reciprocal(rsc[:, 2:3], rsc[:, 2:3]),
                           waits=[a8], inc="dve")
                aA = ph.op("vector",
                           lambda e: e.tensor_tensor(rsc[:, 3:4], rsc[:, 3:4],
                                                     rsc[:, 2:3], OP.mult),
                           waits=[a9], inc="dve")
                aB = ph.op("vector",
                           lambda e: e.tensor_scalar(rsc[:, 28:28 + E], rsc[:, 20:20 + E],
                                                     rsc[:, 3:4], None, OP.mult),
                           waits=[aA], inc="dve")
                rt_prev = ph.op("vector",
                                lambda e, c0=c0: e.scalar_tensor_tensor(
                                    comb[:, c0:c0 + E], rsc[:, 12:12 + E],
                                    rsc[:, 2:3], rsc[:, 28:28 + E],
                                    OP.mult, OP.add),
                                waits=[aB], inc="dve")
        return last_xn, rt_prev

    def attn_a(ph, pst, wo_tok, loc):
        delt = loc["delt"]
        delt2 = loc["delt"]
        Aexp = loc["Aexp"]
        Bu = loc["Bu"]
        h32 = loc["h32"]
        h16 = loc["h16"]
        psA = pst.enter_context(nc.psum_tensor(f"psA_{ph.name}", [NST, S], F32))
        psB = psA
        tdel = []
        for c in range(TC):
            tdel.append(mmgroup(ph, psA[0:1, c * 512:(c + 1) * 512],
                                [(wtT[:, j:j + 1], xnsl(j, c * 512, 512))
                                 for j in range(DT)]))
        tsig = ph.op("scalar",
                     lambda e: e.activation(delt[:, :], psA[0:1, :], AF.Sigmoid,
                                            bias=btcol[0:1, 0:1]),
                     waits=tdel, inc="act")
        tdsc = ph.op("vector",
                     lambda e: e.tensor_scalar(delt2[:, :], delt[:, :], 0.99, 0.01,
                                               OP.mult, OP.add),
                     waits=[tsig], inc="dve")
        tbu = []
        for c in range(TC):
            tbu.append(mmgroup(ph, psB[:, c * 512:(c + 1) * 512],
                               [(wbT[:, j * NST:(j + 1) * NST], xnsl(j, c * 512, 512))
                                for j in range(DT)],
                               waits=[tsig, tdsc]))
        tbue = ph.op("scalar",
                     lambda e: e.activation(Bu[:, :], psB[:, :], AF.Identity,
                                            bias=bbcol[:, 0:1]),
                     waits=tbu + [tsig], inc="act")
        tbc = []
        for c in range(TC):
            tbc.append(mmgroup(ph, psA[:, c * 512:(c + 1) * 512],
                               [(ones16[0:1, 0:NST],
                                 delt2[0:1, c * 512:(c + 1) * 512])],
                               waits=[tsig, tdsc, tbue]))
        texp = ph.op("scalar",
                     lambda e: e.activation(Aexp[:, :], psA[:, :], AF.Exp,
                                            scale=Acol[:, 0:1]),
                     waits=tbc + [tbue], inc="act")
        tscan = ph.op("vector",
                      lambda e: e.tensor_tensor_scan(h32[:, :], Aexp[:, :], Bu[:, :],
                                                     0.0, OP.mult, OP.add),
                      waits=[texp, tbue], inc="dve")
        th16 = ph.op("vector", lambda e: e.tensor_copy(h16[:, :], h32[:, :]),
                     waits=[tscan], inc="dve")
        return th16, texp

    def attn_a2(ph, pst, th16, texp, wo_tok, loc):
        h16 = loc["h16"]
        py = [pst.enter_context(nc.psum_tensor(f"py{i}_{ph.name}", [128, 512], F32))
              for i in range(2)]
        po = [pst.enter_context(nc.psum_tensor(f"po{i}_{ph.name}", [128, 512], F32))
              for i in range(2)]
        ye = [None, None]
        last_ye = None
        idx = 0
        for j in range(DT):
            for c in range(TC):
                b = idx % 2
                tmm = mmgroup(ph, py[b][:, :],
                              [(wcT[:, j * 128:(j + 1) * 128],
                                h16[:, c * 512:(c + 1) * 512])],
                              waits=[th16, ye[b], texp],
                              extra=[(brow["bc"][0:1, j * 128:(j + 1) * 128],
                                      onesb[0:1, 0:512])])
                ye[b] = ph.op("vector",
                              lambda e, j=j, c=c, b=b: e.scalar_tensor_tensor(
                                  bigsl(j * S + c * 512, 512),
                                  xnsl(j, c * 512, 512), Dcol[:, j:j + 1],
                                  py[b][:, :], OP.mult, OP.add),
                              waits=[tmm], inc="dve")
                last_ye = ye[b]
                idx += 1
        xe = [None, None]
        idx = 0
        for t in range(TT):
            for c in range(HC):
                b = idx % 2
                tmm = mmgroup(ph, po[b][:, :],
                              [(bigsl(j * S + t * 128, 128),
                                wfull[:, j * D + c * 512: j * D + c * 512 + 512])
                               for j in range(DT)],
                              waits=[last_ye, xe[b], wo_tok],
                              extra=[(onesb[0:1, 0:128],
                                      brow["boa"][0:1, c * 512:(c + 1) * 512])])
                xe[b] = ph.op("vector",
                              lambda e, t=t, c=c, b=b: e.tensor_tensor(
                                  xsl(t, c * 512, 512), xsl(t, c * 512, 512),
                                  po[b][:, :], OP.add),
                              waits=[tmm], inc="dve")
                idx += 1

    def ffn(ph, pst, rt_tok):
        wg = [lb(pst, "wg", 128, DT * 512, BF16) for i in range(2)]
        wu = [lb(pst, "wu", 128, DT * 512, BF16) for i in range(2)]
        wd = [lb(pst, "wd", 128, 4 * D, BF16)]
        gtmp = [lb(pst, "gtmp", 128, 512, BF16) for i in range(2)]
        psG = [pst.enter_context(nc.psum_tensor(f"psG{i}_{ph.name}", [128, 512], F32))
               for i in range(2)]
        psU = [pst.enter_context(nc.psum_tensor(f"psU{i}_{ph.name}", [128, 512], F32))
               for i in range(2)]
        psD = [pst.enter_context(nc.psum_tensor(f"psD{i}_{ph.name}", [128, 512], F32))
               for i in range(2)]
        NWIN = E * NW
        wtoks = []
        consume = [None] * NWIN
        he = {}
        xe = [None, None]

        def issue_w(win):
            e_, w_ = divmod(win, NW)
            wb = win % 2
            wcons = consume[win - 2] if win >= 2 else None
            gsrc = apS(guT_d, e_ * D * 2 * F + w_ * 512,
                       [[2 * F, 128], [128 * 2 * F, DT], [1, 512]])
            usrc = apS(guT_d, e_ * D * 2 * F + F + w_ * 512,
                       [[2 * F, 128], [128 * 2 * F, DT], [1, 512]])
            sm = f"dgp{wb}"
            ph.dma("gpsimd", wg[wb][:, :], gsrc, waits=[wcons], sem=sm)
            t2 = ph.dma("gpsimd", wu[wb][:, :], usrc, waits=[wcons], sem=sm)
            return t2

        def issue_wd(win):
            e_, w_ = divmod(win, NW)
            dsrc = apS(dwT_d, e_ * F * D + w_ * 512 * D,
                       [[D, 128], [128 * D, 4], [1, D]])
            wcons = consume[win - 1] if win >= 1 else None
            return ph.dma("gpsimd", wd[0][:, :], dsrc, waits=[wcons], sem="dwd")

        for win in range(min(2, NWIN)):
            wtoks.append((issue_w(win), issue_wd(win) if win == 0 else None))
        gidx = 0
        didx = 0
        for e_ in range(E):
            for w_ in range(NW):
                win = e_ * NW + w_
                wb = win % 2
                wt, wdt = wtoks[win]
                if wdt is None:
                    wdt = issue_wd(win)
                last_hact = None
                for i in range(4):
                    for c in range(TC):
                        bg = gidx % 2
                        tg_ = mmgroup(ph, psG[bg][:, :],
                                      [(wg[wb][:, j * 512 + i * 128:
                                               j * 512 + i * 128 + 128],
                                        xnsl(j, c * 512, 512)) for j in range(DT)],
                                      waits=[wt, he.get(("g", gidx - 2))])
                        tu_ = mmgroup(ph, psU[bg][:, :],
                                      [(wu[wb][:, j * 512 + i * 128:
                                               j * 512 + i * 128 + 128],
                                        xnsl(j, c * 512, 512)) for j in range(DT)],
                                      waits=[wt, he.get(("h", gidx - 2))])
                        tge = ph.op("scalar",
                                    lambda e, bg=bg: e.activation(
                                        gtmp[bg][:, :], psG[bg][:, :], AF.Gelu),
                                    waits=[tg_, he.get(("h", gidx - 2))],
                                    inc="act")
                        he[("g", gidx)] = tge
                        th = ph.op("vector",
                                   lambda e, i=i, c=c, bg=bg: e.tensor_tensor(
                                       bigsl(i * S + c * 512, 512), psU[bg][:, :],
                                       gtmp[bg][:, :], OP.mult),
                                   waits=[tu_, tge,
                                          consume[win - 1] if win >= 1 else None],
                                   inc="dve")
                        he[("h", gidx)] = th
                        last_hact = th
                        gidx += 1
                last_mm = None
                for t in range(TT):
                    for hcc in range(HC):
                        bd = didx % 2
                        tdn = mmgroup(ph, psD[bd][:, :],
                                      [(bigsl(i * S + t * 128, 128),
                                        wd[0][:, i * D + hcc * 512:
                                              i * D + hcc * 512 + 512])
                                       for i in range(4)],
                                      waits=[last_hact, wdt, xe[bd]])
                        xe[bd] = ph.op("vector",
                                       lambda e, t=t, hcc=hcc, bd=bd, e_=e_:
                                       e.scalar_tensor_tensor(
                                           xsl(t, hcc * 512, 512), psD[bd][:, :],
                                           comb[:, t * 8 + e_:t * 8 + e_ + 1],
                                           xsl(t, hcc * 512, 512),
                                           OP.mult, OP.add),
                                       waits=[tdn, rt_tok], inc="dve")
                        last_mm = tdn
                        didx += 1
                consume[win] = last_mm
                nxt = win + 2
                if nxt < NWIN:
                    wtoks.append((issue_w(nxt), None))

    def attnb_z(ph, pst, last_xn, wz_tok):
        xn32 = [lb(pst, "zn32", 128, D, F32) for i in range(2)]
        txn = [lb(pst, "ztxn", 128, D, F32) for i in range(2)]
        znb = [lb(pst, "znb", 128, D, BF16) for i in range(2)]
        sqdump = lb(pst, "zsqdump", 128, D, F32)
        gbc = lb(pst, "gbc", 128, D, F32)
        bbc = lb(pst, "bbc", 128, D, F32)
        browf = {nm: lb(pst, "zrow_" + nm, 1, D, F32) for nm in ("lng", "lnb")}
        psZ = [pst.enter_context(nc.psum_tensor(f"psZ{i}_{ph.name}", [128, D], F32))
               for i in range(2)]
        psbz = pst.enter_context(nc.psum_tensor(f"psbz_{ph.name}", [128, 512], F32))
        tgb = None
        trow = {nm: ph.dma("sync", browf[nm][:, :],
                           apS(brow_d[nm], 0, [[D, 1], [1, D]]))
                for nm in ("lng", "lnb")}
        for nm, dst in (("lng", gbc), ("lnb", bbc)):
            for c in range(HC):
                tmm = mmgroup(ph, psbz[:, :],
                              [(onesf[0:1, 0:128],
                                browf[nm][0:1, c * 512:(c + 1) * 512])],
                              waits=[tgb, trow["lnb"]])
                tgb = ph.op("vector",
                            lambda e, dst=dst, c=c: e.tensor_copy(
                                dst[:, c * 512:(c + 1) * 512], psbz[:, :]),
                            waits=[tmm], inc="dve")
        rd = [[], []]
        tsq = None
        tsum = None
        for t in range(TT):
            b = t % 2
            grp = []
            for hc in range(HC):
                grp.append(mmgroup(ph, psZ[b][:, hc * 512:hc * 512 + 512],
                                   [(xnsl(j, t * 128, 128),
                                     wfull[:, j * D + hc * 512: j * D + hc * 512 + 512])
                                    for j in range(DT)],
                                   waits=[last_xn, wz_tok] + rd[b],
                                   extra=[(onesb[0:1, 0:128],
                                           brow["bz"][0:1, hc * 512:hc * 512 + 512])]))
            tsum = ph.op("vector",
                         lambda e, t=t, b=b: e.reduce_sum(zstats[:, t:t + 1],
                                                          psZ[b][:, :], axis=AXX),
                         waits=grp, inc="dve")
            tcp = ph.op("vector",
                        lambda e, t=t, b=b: e.tensor_copy(bigsl(t * D, D),
                                                          psZ[b][:, :]),
                        waits=grp, inc="dve")
            tsq = ph.op("scalar",
                        lambda e, t=t, b=b: e.activation(
                            sqdump[:, :], psZ[b][:, :], AF.Square,
                            accum_out=zstats[:, TT + t:TT + t + 1]),
                        waits=grp + [tsq, tcp], inc="act")
            rd[b] = [tsum, tsq, tcp]
        t1 = ph.op("vector",
                   lambda e: e.tensor_scalar_mul(zstats[:, 2 * TT:3 * TT],
                                                 zstats[:, 0:TT], 1.0 / D),
                   waits=[tsum], inc="dve")
        t2 = ph.op("vector",
                   lambda e: e.tensor_tensor(zstats[:, 3 * TT:4 * TT],
                                             zstats[:, 2 * TT:3 * TT],
                                             zstats[:, 2 * TT:3 * TT], OP.mult),
                   waits=[t1], inc="dve")
        t3 = ph.op("vector",
                   lambda e: e.scalar_tensor_tensor(zstats[:, 3 * TT:4 * TT],
                                                    zstats[:, TT:2 * TT], 1.0 / D,
                                                    zstats[:, 3 * TT:4 * TT],
                                                    OP.mult, OP.subtract),
                   waits=[t2, tsq], inc="dve")
        t4 = ph.op("scalar",
                   lambda e: e.activation(zstats[:, 3 * TT:4 * TT],
                                          zstats[:, 3 * TT:4 * TT],
                                          AF.Sqrt, bias=eps5[:, 0:1]),
                   waits=[t3], inc="act")
        t5 = ph.op("vector",
                   lambda e: e.reciprocal(zstats[:, 3 * TT:4 * TT],
                                          zstats[:, 3 * TT:4 * TT]),
                   waits=[t4], inc="dve")
        psT = [pst.enter_context(nc.psum_tensor(f"psTz{i}_{ph.name}", [128, D], BF16))
               for i in range(2)]
        tr_toks = [None, None]
        ev_toks = [None, None]
        tz3p = [None, None]
        last_zn = None
        for t in range(TT):
            b = t % 2
            tz1 = ph.op("vector",
                        lambda e, t=t, b=b: e.tensor_scalar(
                            xn32[b][:, :], bigsl(t * D, D),
                            zstats[:, 2 * TT + t:2 * TT + t + 1],
                            zstats[:, 3 * TT + t:3 * TT + t + 1],
                            OP.subtract, OP.mult),
                        waits=[t5, tr_toks[b]], inc="dve")
            tz2 = ph.op("vector",
                        lambda e, b=b: e.tensor_tensor(txn[b][:, :], xn32[b][:, :],
                                                       gbc[:, :], OP.mult),
                        waits=[tz1, tz3p[b]], inc="dve")
            tz3 = ph.op("vector",
                        lambda e, b=b: e.tensor_tensor(znb[b][:, :], txn[b][:, :],
                                                       bbc[:, :], OP.add),
                        waits=[tz2, tr_toks[b]], inc="dve")
            tz3p[b] = tz3

            def trf(e, b=b):
                last = None
                for j in range(DT):
                    last = e.transpose(psT[b][:, j * 128:(j + 1) * 128],
                                       znb[b][:, j * 128:(j + 1) * 128],
                                       idbf[:, :])
                return last

            ttr = ph.op("tensor", trf, waits=[tz3, ev_toks[b]], inc="pe")
            tr_toks[b] = ttr
            ev_toks[b] = ph.op("vector",
                               lambda e, t=t, b=b: e.tensor_copy(
                                   apS(xnT, t * 128,
                                       [[DT * S, 128], [S, DT], [1, 128]]),
                                   apS(psT[b], 0,
                                       [[D, 128], [128, DT], [1, 128]])),
                               waits=[ttr], inc="dve")
            last_zn = ev_toks[b]
        return last_zn

    def attnb_qrs(ph, pst, last_zn):
        wqs = [lb(pst, "wqs", 128, DT * 128, BF16)]
        wrs = [lb(pst, "wrs", 128, DT * 128, BF16)]
        whs = [lb(pst, "whs", 128, DT * 128, BF16)]
        Qb = lb(pst, "Qb", 128, S, BF16)
        Rb = lb(pst, "Rb", 128, S, BF16)
        Qc = lb(pst, "Qc", 128, S, BF16)
        Rc = lb(pst, "Rc", 128, S, BF16)
        hcopy = lb(pst, "hcopy", 64, S, BF16)
        stage = lb(pst, "stage", 64, S, BF16)
        hv = lb(pst, "hv", 128, S, BF16)
        scb = [lb(pst, "scb", 128, 512, BF16) for i in range(2)]
        sqb = [lb(pst, "sqb", 128, 512, BF16) for i in range(2)]
        phiF = lb(pst, "phiF", 64, 512, F32)
        targ = [lb(pst, "targ", 64, 512, F32)]
        trig = [lb(pst, "trig", 64, 512, F32)]
        nrm5 = lb(pst, "nrm5", 1, 512, F32)
        rcols = lb(pst, "rcols", 128, TT, F32)
        wq_cons = [None]
        wr_cons = [None]
        wh_cons = [None]
        wtok = [None] * NHP

        def issue_hp(hp):
            sm = "dgp0"
            ph.dma("gpsimd", wqs[0][:, :],
                   apS(wq_d, hp * 128, [[D, 128], [128 * D, DT], [1, 128]]),
                   waits=[wq_cons[0], wr_cons[0], wh_cons[0]], sem=sm)
            ph.dma("gpsimd", wrs[0][:, :],
                   apS(wr_d, hp * 128, [[D, 128], [128 * D, DT], [1, 128]]),
                   sem=sm)
            t3 = ph.dma("gpsimd", whs[0][:, :],
                        apS(wh_d, hp * 128, [[D, 128], [128 * D, DT], [1, 128]]),
                        sem=sm)
            return t3

        wtok[0] = issue_hp(0)
        ote = []
        last_score_mm = None
        first_smm = None
        for hp in range(NHP):
            with contextlib.ExitStack() as pq:
                psQ = [pq.enter_context(nc.psum_tensor(f"psQ{i}_{ph.name}_{hp}",
                                                       [128, 512], F32))
                       for i in range(2)]
                psH = [pq.enter_context(nc.psum_tensor(f"psH{i}_{ph.name}_{hp}",
                                                       [128, 512], F32))
                       for i in range(2)]
                fence = ph.fence()
                qev = [None, None]
                for (dst, wsb, wcol, wti) in ((Qb, wqs, bqcol, 0), (Rb, wrs, brcol, 1)):
                    for tc in range(TC):
                        bb_ = tc % 2
                        tmm = mmgroup(ph, psQ[bb_][:, :],
                                      [(wsb[0][:, j * 128:(j + 1) * 128],
                                        xnsl(j, tc * 512, 512)) for j in range(DT)],
                                      waits=[wtok[hp], last_zn, qev[bb_]] + fence)
                        qev[bb_] = ph.op("scalar",
                                         lambda e, dst=dst, tc=tc, bb_=bb_, wcol=wcol,
                                         hp=hp, psQ=psQ:
                                         e.activation(dst[:, tc * 512:(tc + 1) * 512],
                                                      psQ[bb_][:, :], AF.Identity,
                                                      bias=wcol[:, hp:hp + 1]),
                                         waits=[tmm], inc="act")
                    if wti == 0:
                        wq_cons[0] = tmm
                    else:
                        wr_cons[0] = tmm
                hev = [None, None]
                for t4 in range(TT // 4):
                    bb_ = t4 % 2

                    def hvf(e, t4=t4, bb_=bb_, hp=hp, psH=psH):
                        last = None
                        for tl in range(4):
                            t = t4 * 4 + tl
                            for j in range(DT):
                                last = e.matmul(psH[bb_][:, tl * 128:(tl + 1) * 128],
                                                xnsl(j, t * 128, 128),
                                                whs[0][:, j * 128:(j + 1) * 128],
                                                start=(j == 0), stop=False)
                            last = e.matmul(psH[bb_][:, tl * 128:(tl + 1) * 128],
                                            onesb[0:1, 0:128],
                                            brow["bh"][0:1, hp * 128:hp * 128 + 128],
                                            start=False, stop=True)
                        return last

                    tmm = ph.op("tensor", hvf,
                                waits=[wtok[hp], last_zn, hev[bb_]] + fence,
                                inc="pe")
                    wh_cons[0] = tmm
                    hev[bb_] = ph.op("vector",
                                     lambda e, t4=t4, bb_=bb_, psH=psH: e.tensor_copy(
                                         hv[:, t4 * 512:(t4 + 1) * 512],
                                         psH[bb_][:, :]),
                                     waits=[tmm], inc="dve")
            if hp + 1 < NHP:
                wtok[hp + 1] = issue_hp(hp + 1)
            fence_h = ph.fence()
            with contextlib.ExitStack() as pq:
                psO = pq.enter_context(nc.psum_tensor(f"psO_{ph.name}_{hp}",
                                                      [128, TT * 64], F32))
                psN = pq.enter_context(nc.psum_tensor(f"psN_{ph.name}_{hp}",
                                                      [1, S], F32))
                hc_tok = None
                st_prev = None
                for h2 in range(2):
                    pq_p = contextlib.ExitStack()
                    psP = [pq_p.enter_context(
                        nc.psum_tensor(f"psP{i}_{ph.name}_{hp}_{h2}",
                                       [64, 512], F32)) for i in range(2)]
                    tok_phi = [None, None]
                    tok_sin = [None]
                    tok_tt = [None]
                    tok_cpy = None
                    asm = []
                    pidx = 0
                    for (srcB, dstC) in ((Qb, Qc), (Rb, Rc)):
                        if h2 == 0:
                            srcA = srcB
                            roff = 0
                        else:
                            hc_tok = ph.dma("sync", hcopy[:, :],
                                            srcB[64:128, :],
                                            waits=[first_smm, tok_tt[0], st_prev,
                                                   qev[0], qev[1]],
                                            sem="dact")
                            srcA = hcopy
                            roff = 0
                        sin_toks = []
                        for tc in range(TC):
                            bb_ = pidx % 2
                            tmm = mmgroup(ph, psP[bb_][:, :],
                                          [(wphi[0:64, :],
                                            srcA[roff:roff + 64,
                                                 tc * 512:(tc + 1) * 512])],
                                          waits=[qev[0], qev[1], tok_phi[bb_],
                                                 last_score_mm, hc_tok, ote]
                                          + fence_h)
                            tcpy = ph.op("scalar",
                                         lambda e, bb_=bb_, psP=psP: e.activation(
                                             phiF[:, :], psP[bb_][:, :],
                                             AF.Identity, bias=bphic[:, 0:1]),
                                         waits=[tmm, tok_cpy, tok_sin[0],
                                                tok_tt[0]], inc="act")
                            tok_phi[bb_] = tcpy
                            tok_cpy = tcpy
                            w1 = ph.op("vector",
                                       lambda e: e.tensor_scalar(
                                           targ[0][:, :], phiF[:, :], PI, None,
                                           OP.is_gt),
                                       waits=[tcpy, tok_sin[0], tok_tt[0]],
                                       inc="dve")
                            w2 = ph.op("vector",
                                       lambda e: e.tensor_scalar(
                                           trig[0][:, :], phiF[:, :], -PI, None,
                                           OP.is_lt),
                                       waits=[w1, tok_sin[0]], inc="dve")
                            w3 = ph.op("vector",
                                       lambda e: e.tensor_tensor(
                                           targ[0][:, :], trig[0][:, :],
                                           targ[0][:, :], OP.subtract),
                                       waits=[w2], inc="dve")
                            w4 = ph.op("vector",
                                       lambda e: e.scalar_tensor_tensor(
                                           trig[0][:, :], targ[0][:, :], 2 * PI,
                                           phiF[:, :], OP.mult, OP.add),
                                       waits=[w3], inc="dve")
                            tsn = ph.op("scalar",
                                        lambda e: e.activation(
                                            targ[0][:, :], trig[0][:, :], AF.Sin),
                                        waits=[w4], inc="act")
                            ttt = ph.op("vector",
                                        lambda e, srcA=srcA, roff=roff, tc=tc:
                                        e.tensor_tensor(
                                            stage[:, tc * 512:(tc + 1) * 512],
                                            srcA[roff:roff + 64,
                                                 tc * 512:(tc + 1) * 512],
                                            targ[0][:, :], OP.mult),
                                        waits=[tsn, st_prev], inc="dve")
                            sin_toks.append(ttt)
                            ab = ph.op("scalar",
                                       lambda e: e.activation(
                                           phiF[:, :], trig[0][:, :], AF.Abs),
                                       waits=[tsn, w4], inc="act")
                            tcs = ph.op("scalar",
                                        lambda e: e.activation(
                                            targ[0][:, :], phiF[:, :], AF.Sin,
                                            scale=-1.0, bias=pihalf[0:64, 0:1]),
                                        waits=[ab, ttt], inc="act")
                            tok_sin[0] = tcs
                            tt2 = ph.op("vector",
                                        lambda e, srcA=srcA, roff=roff, dstC=dstC,
                                        tc=tc:
                                        e.tensor_tensor(
                                            dstC[0:64, tc * 512:(tc + 1) * 512],
                                            srcA[roff:roff + 64,
                                                 tc * 512:(tc + 1) * 512],
                                            targ[0][:, :], OP.mult),
                                        waits=[tcs, last_score_mm], inc="dve")
                            tok_tt[0] = tt2
                            pidx += 1
                        st_prev = ph.dma("sync", dstC[64:128, :], stage[:, :],
                                         waits=sin_toks + [first_smm],
                                         sem="dact")
                        asm.append(st_prev)
                    qc_done = [tok_tt[0]] + asm
                    pq_p.close()
                    pq_s = contextlib.ExitStack()
                    psS = [pq_s.enter_context(
                        nc.psum_tensor(f"psS{i}_{ph.name}_{hp}_{h2}",
                                       [128, 512], F32)) for i in range(2)]
                    fence2 = ph.fence()
                    sce = [None, None]
                    sqe = [None, None]
                    prev_no = [(None, None), (None, None)]
                    prev = None
                    last_norm = None
                    last_out = None
                    kidx = 0
                    first_smm = None

                    def emit_no(kt_, qc_, bbp, tscp, tsqp, h2=h2,
                                psN=psN, psO=psO):
                        def nf(e, kt_=kt_, qc_=qc_, bbp=bbp, psN=psN):
                            return e.matmul(
                                psN[0:1, qc_ * 512:(qc_ + 1) * 512],
                                onescol[:, 0:1], sqb[bbp][:, :],
                                start=(kt_ == 0), stop=(kt_ == TKT - 1))

                        tno = ph.op("tensor", nf, waits=[tsqp], inc="pe")

                        def outf(e, kt_=kt_, qc_=qc_, bbp=bbp, h2=h2, psO=psO):
                            last = None
                            for qtl in range(4):
                                qt = qc_ * 4 + qtl
                                stt_ = (kt_ == 0 and qt % 8 == 0)
                                stp_ = (kt_ == TKT - 1 and
                                        (qt % 8 == 7 or qt == TT - 1))
                                last = e.matmul(
                                    psO[:, qt * 64:(qt + 1) * 64],
                                    scb[bbp][:, qtl * 128:(qtl + 1) * 128],
                                    hv[:, kt_ * 128 + h2 * 64:
                                       kt_ * 128 + h2 * 64 + 64],
                                    start=stt_, stop=stp_)
                            return last

                        tout = ph.op("tensor", outf, waits=[tscp], inc="pe")
                        prev_no[bbp] = (tno, tout)
                        return tno, tout

                    for qc in range(TC):
                        for kt in range(TKT):
                            bb_ = kidx % 2
                            tmm = mmgroup(ph, psS[bb_][:, :],
                                          [(Rc[:, kt * 128:(kt + 1) * 128],
                                            Qc[:, qc * 512:(qc + 1) * 512])],
                                          waits=qc_done + [sce[bb_], sqe[bb_], ote]
                                          + fence2)
                            last_score_mm = tmm
                            if first_smm is None:
                                first_smm = tmm
                            if prev is not None:
                                last_norm, last_out = emit_no(*prev)
                            tsc = ph.op("vector",
                                        lambda e, bb_=bb_, psS=psS: e.tensor_copy(
                                            scb[bb_][:, :], psS[bb_][:, :]),
                                        waits=[tmm, prev_no[bb_][1]], inc="dve")
                            tsq_ = ph.op("scalar",
                                         lambda e, bb_=bb_: e.activation(
                                             sqb[bb_][:, :], scb[bb_][:, :],
                                             AF.Square),
                                         waits=[tsc, prev_no[bb_][0]], inc="act")
                            sce[bb_], sqe[bb_] = tsc, tsq_
                            prev = (kt, qc, bb_, tsc, tsq_)
                            kidx += 1
                    last_norm, last_out = emit_no(*prev)
                    trp_toks = []
                    tnr = None
                    for qc in range(TC):
                        tn1 = ph.op("vector",
                                    lambda e, qc=qc, psN=psN: e.tensor_copy(
                                        nrm5[:, :],
                                        psN[0:1, qc * 512:(qc + 1) * 512]),
                                    waits=[last_norm, tnr] + trp_toks[-1:],
                                    inc="dve")
                        tn2 = ph.op("scalar",
                                    lambda e: e.activation(nrm5[:, :], nrm5[:, :],
                                                           AF.Sqrt),
                                    waits=[tn1], inc="act")
                        tn3 = ph.op("vector",
                                    lambda e: e.tensor_scalar(nrm5[:, :], nrm5[:, :],
                                                              1e-12, None, OP.max),
                                    waits=[tn2], inc="dve")
                        tnr = ph.op("vector",
                                    lambda e: e.reciprocal(nrm5[:, :], nrm5[:, :]),
                                    waits=[tn3], inc="dve")

                        def trp(e, qc=qc, psS=psS):
                            last = None
                            for i in range(4):
                                last = e.transpose(
                                    psS[0][:, qc * 4 + i:qc * 4 + i + 1],
                                    nrm5[0:1, i * 128:(i + 1) * 128],
                                    id32[0:1, 0:1])
                            return last

                        tw_ = [tnr, sce[0], sce[1], sqe[0], sqe[1]] if qc == 0                             else [tnr]
                        trp_toks.append(ph.op("tensor", trp, waits=tw_, inc="pe"))
                    tn5 = ph.op("vector",
                                lambda e, psS=psS: e.tensor_copy(rcols[:, 0:TT],
                                                                 psS[0][:, 0:TT]),
                                waits=trp_toks, inc="dve")
                    ote = []
                    for qt in range(TT):
                        ote.append(ph.op("vector",
                                         lambda e, qt=qt, hp=hp, h2=h2, psO=psO:
                                         e.tensor_scalar(
                                             bigsl(qt * D + hp * 128 + h2 * 64, 64),
                                             psO[:, qt * 64:(qt + 1) * 64],
                                             rcols[:, qt:qt + 1], None, OP.mult),
                                         waits=[tn5, last_out], inc="dve"))
                    pq_s.close()
        return ote

    def attnb_wo(ph, pst, ot_toks, wfull_prev):
        wo_tok = ph.dma("gpsimd", wfull[:, :],
                        apS(wob_d, 0, [[D, 128], [128 * D, DT], [1, D]]),
                        waits=[wfull_prev[0]], sem="dwf")
        wfull_prev[0] = wo_tok
        psT = [pst.enter_context(nc.psum_tensor(f"psTo{i}_{ph.name}", [128, D], BF16))
               for i in range(2)]
        psF = [pst.enter_context(nc.psum_tensor(f"psF{i}_{ph.name}", [128, 512], F32))
               for i in range(2)]
        ev = [None, None]
        last_ev = None
        for qt in range(TT):
            b = qt % 2

            def trf(e, qt=qt, b=b):
                last = None
                for j in range(DT):
                    last = e.transpose(psT[b][:, j * 128:(j + 1) * 128],
                                       bigsl(qt * D + j * 128, 128), idbf[:, :])
                return last

            ttr = ph.op("tensor", trf, waits=ot_toks + [ev[b]], inc="pe")
            ev[b] = ph.op("vector",
                          lambda e, qt=qt, b=b: e.tensor_copy(
                              apS(xnT, qt * 128, [[DT * S, 128], [S, DT], [1, 128]]),
                              apS(psT[b], 0, [[D, 128], [128, DT], [1, 128]])),
                          waits=[ttr], inc="dve")
            last_ev = ev[b]
        xe = [None, None]
        idx = 0
        for t in range(TT):
            for c in range(HC):
                b = idx % 2
                tmm = mmgroup(ph, psF[b][:, :],
                              [(xnsl(j, t * 128, 128),
                                wfull[:, j * D + c * 512: j * D + c * 512 + 512])
                               for j in range(DT)],
                              waits=[last_ev, xe[b], wo_tok],
                              extra=[(onesb[0:1, 0:128],
                                      brow["bob"][0:1, c * 512:(c + 1) * 512])])
                xe[b] = ph.op("vector",
                              lambda e, t=t, c=c, b=b: e.tensor_tensor(
                                  xsl(t, c * 512, 512), xsl(t, c * 512, 512),
                                  psF[b][:, :], OP.add),
                              waits=[tmm], inc="dve")
                idx += 1

    store_toks = []
    prev_store = None
    wfull_prev = [None]
    for batch in range(NB):
        ph = new_ph(f"b{batch}_load")
        src = apS(x_d, batch * S * D, [[D, 128], [128 * D, TT], [1, D]])
        dst = apS(x_sb, 0, [[TT * D, 128], [D, TT], [1, D]])
        xl = ph.dma("sync", dst, src, waits=[prev_store])

        ph = new_ph(f"b{batch}_ln1")
        with contextlib.ExitStack() as pst:
            woa_tok = ph.dma("gpsimd", wfull[:, :],
                             apS(woTa_d, 0, [[D, 128], [128 * D, DT], [1, D]]),
                             waits=[wfull_prev[0]], sem="dwf")
            wfull_prev[0] = woa_tok
            ln_phase(ph, pst, router=False, wait_tokens=[xl])

        if "attn_a" in enable:
            ph = new_ph(f"b{batch}_attna")
            with contextlib.ExitStack() as loc_st:
                loc = {
                    "delt": lb(loc_st, "delt", 1, S, F32),
                    "Aexp": lb(loc_st, "Aexp", NST, S, F32),
                    "Bu": lb(loc_st, "Bu", NST, S, F32),
                    "h32": lb(loc_st, "h32", NST, S, F32),
                    "h16": lb(loc_st, "h16", NST, S, BF16),
                }
                with contextlib.ExitStack() as pst:
                    th16, texp = attn_a(ph, pst, woa_tok, loc)
                with contextlib.ExitStack() as pst:
                    attn_a2(ph, pst, th16, texp, woa_tok, loc)

        if "ffn1" in enable or "router" in enable:
            ph = new_ph(f"b{batch}_ln2")
            with contextlib.ExitStack() as pst:
                _, rt = ln_phase(ph, pst, router=True)
            if "ffn1" in enable:
                ph = new_ph(f"b{batch}_ffn1")
                with contextlib.ExitStack() as pst:
                    ffn(ph, pst, rt)

        if "attn_b" in enable:
            ph = new_ph(f"b{batch}_ln3")
            with contextlib.ExitStack() as pst:
                wz_tok = ph.dma("gpsimd", wfull[:, :],
                                apS(wz_d, 0, [[D, 128], [128 * D, DT], [1, D]]),
                                waits=[wfull_prev[0]], sem="dwf")
                wfull_prev[0] = wz_tok
                last_xn, _ = ln_phase(ph, pst, router=False)
            ph = new_ph(f"b{batch}_z")
            with contextlib.ExitStack() as pst:
                last_zn = attnb_z(ph, pst, last_xn, wz_tok)
            ph = new_ph(f"b{batch}_qrs")
            with contextlib.ExitStack() as pst:
                ot_toks = attnb_qrs(ph, pst, last_zn)
            ph = new_ph(f"b{batch}_wo")
            with contextlib.ExitStack() as pst:
                attnb_wo(ph, pst, ot_toks, wfull_prev)

        if "ffn2" in enable:
            ph = new_ph(f"b{batch}_ln4")
            with contextlib.ExitStack() as pst:
                _, rt = ln_phase(ph, pst, router=True)
            ph = new_ph(f"b{batch}_ffn2")
            with contextlib.ExitStack() as pst:
                ffn(ph, pst, rt)

        ph = new_ph(f"b{batch}_store")
        dst = apS(out_d, batch * S * D, [[D, 128], [128 * D, TT], [1, D]])
        src = apS(x_sb, 0, [[TT * D, 128], [D, TT], [1, D]])
        prev_store = ph.dma("sync", dst, src)
        store_toks.append(prev_store)

    ph = new_ph("final")
    ph.op("sync", lambda e: e.nop(), waits=store_toks)

    for ph_ in phases:
        ph_.emit()
    k.close()
    st.close()
    nc.finalize()
    return nc


def host_prep(cfg, inputs):
    DT = cfg["D"] // 128
    f32 = np.float32
    bf = ml_dtypes.bfloat16
    g = lambda n: np.asarray(inputs[n], dtype=f32)
    shared = {
        "guT": np.ascontiguousarray(g("L0_ffn_gate_up_w").transpose(0, 2, 1)),
        "dwT": np.ascontiguousarray(g("L0_ffn_down_w").transpose(0, 2, 1)),
        "gwT": np.ascontiguousarray(g("L0_ffn_gate_w").T),
        "woTa": np.ascontiguousarray(g("L0_attn_a_W_o").T),
        "wbT": np.ascontiguousarray(g("L0_attn_a_W_b").T),
        "wtT": np.ascontiguousarray(g("L0_attn_a_W_t").reshape(-1, 128)),
        "wcT": np.ascontiguousarray(g("L0_attn_a_W_c").T),
        "A": g("L0_attn_a_A"),
        "bb": g("L0_attn_a_b_b"),
        "bt": g("L0_attn_a_b_t"),
        "b_bc": g("L0_attn_a_b_c"),
        "b_boa": g("L0_attn_a_b_o"),
        "Dg": np.ascontiguousarray(g("L0_attn_a_D").reshape(DT, 128)),
        "wz": g("L0_attn_b_Wz"),
        "wq": g("L0_attn_b_Wq"),
        "wr": g("L0_attn_b_Wr"),
        "wh": g("L0_attn_b_Wh"),
        "wob": g("L0_attn_b_Wo"),
        "b_bz": g("L0_attn_b_bz"),
        "bq": np.ascontiguousarray(g("L0_attn_b_bq").reshape(DT, 128)),
        "br": np.ascontiguousarray(g("L0_attn_b_br").reshape(DT, 128)),
        "b_bh": g("L0_attn_b_bh"),
        "b_bob": g("L0_attn_b_bo"),
        "b_lng": g("L0_attn_b_ln_g"),
        "b_lnb": g("L0_attn_b_ln_b"),
        "wphi": g("L0_attn_b_Wphi"),
        "bphi": g("L0_attn_b_bphi"),
        "id32": np.eye(128, dtype=f32),
        "idbf": np.eye(128, dtype=bf),
        "ones_f": np.ones((1, 128), dtype=f32),
        "ones_bf": np.ones((1, 512), dtype=bf),
        "ones16": np.ones((1, 16), dtype=f32),
    }
    return shared


def kernel(**inputs):
    cfg = CFG_FULL
    NB = cfg["BATCHES"]
    x = np.asarray(inputs["L0_attn_a_x"], dtype=np.float32)
    n_cores = 8
    shared = host_prep(cfg, inputs)
    nc = build(cfg)
    in_maps = []
    for c in range(n_cores):
        m = dict(shared)
        m["x"] = np.ascontiguousarray(x[c * NB:(c + 1) * NB])
        in_maps.append(m)
    res = run_bass_kernel_spmd(nc, in_maps, core_ids=list(range(n_cores)))
    outs = [np.asarray(r["out"]) for r in res.results]
    return np.concatenate(outs, axis=0).astype(np.float32)


# version 06
# speedup vs baseline: 1.0702x; 1.0702x over previous
import contextlib
import numpy as np
import ml_dtypes
import concourse.bass as bass
import concourse.mybir as mybir
from concourse.bass_utils import run_bass_kernel_spmd
from concourse.alu_op_type import AluOpType as OP

F32 = mybir.dt.float32
BF16 = mybir.dt.bfloat16
AF = mybir.ActivationFunctionType
AXX = mybir.AxisListType.X
ENGINES = ("tensor", "vector", "scalar", "gpsimd", "sync")
PI = float(np.pi)

CFG_FULL = dict(S=2048, D=1024, NST=16, F=4096, E=8, BATCHES=2)
ROUTER_LITE = [0]
ALL_PARTS = ("attn_a", "ffn1", "attn_b", "ffn2")


class K:
    def __init__(self, nc):
        self.nc = nc
        self.sems = {}
        self.shadow = {}
        self.stack = contextlib.ExitStack()
        for name in ("pe", "dve", "act", "pool", "dsync", "dgp", "dact",
                     "dgp0", "dgp1", "dwf", "dwd", "dinit"):
            self.sems[name] = self.stack.enter_context(nc.semaphore(name))
            self.shadow[name] = 0
        self.compute_sems = ("pe", "dve", "act", "pool")

    def close(self):
        self.stack.close()


class Ph:

    def __init__(self, k, name=""):
        self.k = k
        self.name = name
        self.ops = {e: [] for e in ENGINES}
        self.base = {s: k.shadow[s] for s in k.compute_sems}

    def op(self, eng, fn, waits=(), inc=None, n=1):
        tok = None
        if inc is not None:
            self.k.shadow[inc] += n
            tok = (inc, self.k.shadow[inc])
        wl = []
        for w in waits:
            if w is None:
                continue
            if isinstance(w, list):
                wl.extend(x for x in w if x is not None)
            else:
                wl.append(w)
        self.ops[eng].append((fn, wl, inc, n))
        return tok

    def dma(self, eng, dst, src, waits=(), sem=None):
        if sem is None:
            sem = {"sync": "dsync", "gpsimd": "dgp", "scalar": "dact"}[eng]
        return self.op(eng, lambda e, d=dst, s=src: e.dma_start(d, s),
                       waits=waits, inc=sem, n=16)

    def fence(self):
        return [("dve", self.k.shadow["dve"]), ("act", self.k.shadow["act"])]

    def emit(self):
        k = self.k
        with k.nc.Block(self.name or None) as block:
            for eng in ENGINES:
                if not self.ops[eng]:
                    continue

                def body(e, ops=self.ops[eng], base=dict(self.base)):
                    hw = base
                    for fn, waits, inc, n in ops:
                        agg = {}
                        for (sem, val) in waits:
                            if val > agg.get(sem, 0):
                                agg[sem] = val
                        for sem, val in agg.items():
                            if hw.get(sem, 0) >= val:
                                continue
                            e.wait_ge(k.sems[sem], val)
                            hw[sem] = val
                        instr = fn(e)
                        if inc is not None:
                            instr.then_inc(k.sems[inc], n)

                getattr(block, eng)(body)


def build(cfg, enable=ALL_PARTS):
    S, D, NST, F, E, NB = (cfg["S"], cfg["D"], cfg["NST"], cfg["F"],
                           cfg["E"], cfg["BATCHES"])
    TT, DT, TC, HC = S // 128, D // 128, S // 512, D // 512
    NW = F // 512
    NHP = D // 128
    TKT = TT
    assert S % 512 == 0 and D % 512 == 0 and F % 512 == 0 and DT >= 4

    nc = bass.Bass("TRN2", target_bir_lowering=False)
    k = K(nc)

    def din(name, shape, dt=F32):
        return nc.dram_tensor(name, list(shape), dt, kind="ExternalInput")

    x_d = din("x", (NB, S, D))
    out_d = nc.dram_tensor("out", [NB, S, D], F32, kind="ExternalOutput")
    guT_d = din("guT", (E, D, 2 * F))
    dwT_d = din("dwT", (E, F, D))
    gwT_d = din("gwT", (D, E))
    woTa_d = din("woTa", (D, D))
    wbT_d = din("wbT", (D, NST))
    wtT_d = din("wtT", (DT, 128))
    wcT_d = din("wcT", (NST, D))
    A_d = din("A", (NST,))
    bb_d = din("bb", (NST,))
    bt_d = din("bt", (1,))
    Dg_d = din("Dg", (DT, 128))
    wz_d = din("wz", (D, D))
    wq_d = din("wq", (D, D))
    wr_d = din("wr", (D, D))
    wh_d = din("wh", (D, D))
    wob_d = din("wob", (D, D))
    bq_d = din("bq", (128, 16))
    br_d = din("br", (128, 16))
    wphi_d = din("wphi", (64, 128))
    bphi_d = din("bphi", (128, 1))
    id32_d = din("id32", (128, 128))
    idbf_d = din("idbf", (128, 128), BF16)
    ones_d = din("ones_f", (1, 128))
    onesbf_d = din("ones_bf", (1, 512), BF16)
    ones16_d = din("ones16", (1, 16))
    brow_d = {nm: din("b_" + nm, (D,))
              for nm in ("bc", "boa", "bz", "bh", "bob", "lng", "lnb")}

    st = contextlib.ExitStack()
    sb = lambda name, p, f, dt: st.enter_context(nc.sbuf_tensor(name, [p, f], dt))
    _uid = [0]

    def lb(pst, name, p, f, dt):
        _uid[0] += 1
        return pst.enter_context(nc.sbuf_tensor(f"{name}_{_uid[0]}", [p, f], dt))

    x_sb = sb("x_sb", 128, TT * D, F32)
    xnT = sb("xnT", 128, DT * S, BF16)
    big = sb("big", 128, TT * D, BF16)
    wfull = sb("wfull", 128, DT * D, BF16)
    stats = sb("stats", 128, 4 * TT, F32)
    zstats = sb("zstats", 128, 4 * TT, F32)
    rsc = sb("rsc", 128, 40, F32)
    comb = sb("comb", 128, TT * 8, F32)
    id32 = sb("id32s", 128, 128, F32)
    idbf = sb("idbfs", 128, 128, BF16)
    onesf = sb("onesf", 1, 128, F32)
    onesb = sb("onesb", 1, 512, BF16)
    ones16 = sb("ones16s", 1, 16, F32)
    onescol = sb("onescol", 128, 1, BF16)
    eps5 = sb("eps5", 128, 1, F32)
    Acol = sb("Acol", NST, 1, F32)
    bbcol = sb("bbcol", NST, 1, F32)
    btcol = sb("btcol", 1, 1, F32)
    biascol = sb("biascol", 128, 1, F32)
    Dcol = sb("Dcol", 128, DT, F32)
    bqdup = sb("bqdup", 128, 16, F32)
    brdup = sb("brdup", 128, 16, F32)
    rows8 = sb("rows8", DT, 128, F32)
    wbT = sb("wbT_s", 128, DT * NST, BF16)
    wtT = sb("wtT_s", 128, DT, BF16)
    wcT = sb("wcT_s", NST, D, BF16)
    gwT = sb("gwT_s", 128, DT * E, F32)
    wphi2 = sb("wphi2_s", 64, 128, BF16)
    brow = {nm: sb(f"brow_{nm}", 1, D, BF16)
            for nm in ("bc", "boa", "bz", "bh", "bob")}

    phases = []

    def new_ph(name):
        ph = Ph(k, name)
        phases.append(ph)
        return ph

    def apS(t, off, dims):
        return bass.AP(t, off, dims)

    def xsl(t, lo=0, wdt=None):
        wdt = D if wdt is None else wdt
        return x_sb[:, t * D + lo: t * D + lo + wdt]

    def xnsl(j, lo, wdt):
        return xnT[:, j * S + lo: j * S + lo + wdt]

    def bigsl(lo, wdt):
        return big[:, lo:lo + wdt]

    def mmgroup(ph, out_ap, pieces, waits=(), extra=()):
        allp = list(pieces) + list(extra)

        def fn(e, allp=allp, out_ap=out_ap):
            n = len(allp)
            last = None
            for i, (l, r) in enumerate(allp):
                last = e.matmul(out_ap, l, r, start=(i == 0), stop=(i == n - 1))
            return last

        return ph.op("tensor", fn, waits=waits, inc="pe")

    ph = new_ph("setup")
    td = []
    td.append(ph.dma("sync", id32[:, :], id32_d.ap()))
    td.append(ph.dma("sync", idbf[:, :], idbf_d.ap()))
    td.append(ph.dma("sync", onesf[:, :], ones_d.ap()))
    td.append(ph.dma("sync", onesb[:, :], onesbf_d.ap()))
    td.append(ph.dma("sync", ones16[:, :], ones16_d.ap()))
    td.append(ph.dma("sync", Acol[:, 0:1], apS(A_d, 0, [[1, NST], [1, 1]])))
    td.append(ph.dma("sync", bbcol[:, 0:1], apS(bb_d, 0, [[1, NST], [1, 1]])))
    td.append(ph.dma("sync", btcol[:, 0:1], apS(bt_d, 0, [[1, 1], [1, 1]])))
    tbph = ph.dma("sync", biascol[:, 0:1], bphi_d.ap(), sem="dinit")
    tbq = ph.dma("sync", bqdup[:, :], bq_d.ap(), sem="dinit")
    tbr = ph.dma("sync", brdup[:, :], br_d.ap(), sem="dinit")
    td.extend([tbph, tbq, tbr])
    for nm in brow:
        td.append(ph.dma("gpsimd", brow[nm][:, :],
                         apS(brow_d[nm], 0, [[D, 1], [1, D]])))
    td.append(ph.dma("sync", gwT[:, :],
                     apS(gwT_d, 0, [[E, 128], [128 * E, DT], [1, E]])))
    td.append(ph.dma("gpsimd", wbT[:, :],
                     apS(wbT_d, 0, [[NST, 128], [128 * NST, DT], [1, NST]])))
    td.append(ph.dma("gpsimd", wphi2[:, :], wphi_d.ap()))
    td.append(ph.dma("gpsimd", wcT[:, :], wcT_d.ap()))
    ph.op("vector", lambda e: e.memset(eps5[:, 0:1], 1e-5), inc="dve")
    ph.op("vector", lambda e: e.memset(onescol[:, 0:1], 1.0), inc="dve")
    ph.op("vector", lambda e: e.tensor_scalar(biascol[0:64, 0:1], biascol[0:64, 0:1],
                                              PI / 2, None, OP.add),
          waits=[tbph, tbq, tbr], inc="dve")
    tprev = None
    setup_ps = contextlib.ExitStack()
    ps_c = setup_ps.enter_context(nc.psum_tensor("ps_c", [128, DT], F32))
    trd = None
    for dd, dst in ((Dg_d, Dcol), (wtT_d, wtT)):
        tl = ph.dma("scalar", rows8[0:DT, :], dd.ap(), waits=[trd])
        tt = ph.op("tensor",
                   lambda e: e.transpose(ps_c[:, :], rows8[0:DT, :], id32[0:DT, 0:DT]),
                   waits=[tl, trd] + td, inc="pe")
        trd = ph.op("vector",
                    lambda e, dst=dst: e.tensor_copy(dst[:, :], ps_c[:, :]),
                    waits=[tt], inc="dve")
    setup_ps.close()

    def ln_phase(ph, pst, router, wait_tokens=()):
        xn32 = [lb(pst, "xn32", 128, D, F32) for i in range(2)]
        txn = [lb(pst, "txn", 128, D, F32) for i in range(2)] if router else None
        sqdump = lb(pst, "sqdump", 128, D, F32)
        psT = [pst.enter_context(nc.psum_tensor(f"psT{i}_{ph.name}", [128, D], F32))
               for i in range(2)]
        psr = ([pst.enter_context(nc.psum_tensor(f"psr{i}_{ph.name}", [128, 8], F32))
                for i in range(2)] if router else None)
        tsum = tsq = None
        for t in range(TT):
            tsum = ph.op("vector",
                         lambda e, t=t: e.reduce_sum(stats[:, t:t + 1], xsl(t), axis=AXX),
                         waits=list(wait_tokens),
                         inc="dve" if t == TT - 1 else None)
            tsq = ph.op("scalar",
                        lambda e, t=t: e.activation(
                            sqdump[:, :], xsl(t), AF.Square,
                            accum_out=stats[:, TT + t:TT + t + 1]),
                        waits=list(wait_tokens) + [tsq], inc="act")
        t1 = ph.op("vector",
                   lambda e: e.tensor_scalar_mul(stats[:, 2 * TT:3 * TT],
                                                 stats[:, 0:TT], 1.0 / D),
                   waits=[tsum], inc="dve")
        t2 = ph.op("vector",
                   lambda e: e.tensor_tensor(stats[:, 3 * TT:4 * TT],
                                             stats[:, 2 * TT:3 * TT],
                                             stats[:, 2 * TT:3 * TT], OP.mult),
                   waits=[t1], inc="dve")
        t3 = ph.op("vector",
                   lambda e: e.scalar_tensor_tensor(stats[:, 3 * TT:4 * TT],
                                                    stats[:, TT:2 * TT], 1.0 / D,
                                                    stats[:, 3 * TT:4 * TT],
                                                    OP.mult, OP.subtract),
                   waits=[t2, tsq], inc="dve")

        t4 = ph.op("scalar",
                   lambda e: e.activation(stats[:, 3 * TT:4 * TT],
                                          stats[:, 3 * TT:4 * TT],
                                          AF.Sqrt, bias=eps5[:, 0:1]),
                   waits=[t3], inc="act")
        t5 = ph.op("vector",
                   lambda e: e.reciprocal(stats[:, 3 * TT:4 * TT],
                                          stats[:, 3 * TT:4 * TT]),
                   waits=[t4], inc="dve")
        ev_toks = [None, None]
        tr_toks = [None, None]
        cp_toks = [None, None]
        rt_prev = None
        last_xn = None
        for t in range(TT):
            b = t % 2
            tts = ph.op("vector",
                        lambda e, t=t, b=b: e.tensor_scalar(
                            xn32[b][:, :], xsl(t),
                            stats[:, 2 * TT + t:2 * TT + t + 1],
                            stats[:, 3 * TT + t:3 * TT + t + 1],
                            OP.subtract, OP.mult),
                        waits=[t5, tr_toks[b]], inc="dve")

            def trf(e, b=b):
                last = None
                for j in range(DT):
                    last = e.transpose(psT[b][:, j * 128:(j + 1) * 128],
                                       xn32[b][:, j * 128:(j + 1) * 128],
                                       id32[:, :])
                return last

            ttr = ph.op("tensor", trf,
                        waits=[tts, ev_toks[b], cp_toks[b]], inc="pe")
            tr_toks[b] = ttr
            ev_toks[b] = ph.op("vector",
                               lambda e, t=t, b=b: e.tensor_copy(
                                   apS(xnT, t * 128,
                                       [[DT * S, 128], [S, DT], [1, 128]]),
                                   apS(psT[b], 0,
                                       [[D, 128], [128, DT], [1, 128]])),
                               waits=[ttr], inc="dve")
            last_xn = ev_toks[b]
            if router:
                cp_toks[b] = ph.op("scalar",
                                   lambda e, b=b: e.copy(txn[b][:, :], psT[b][:, :]),
                                   waits=[ttr, ev_toks[b]], inc="act")
                if ROUTER_LITE[0] >= 2:
                    rt_prev = ph.op("vector",
                                    lambda e, t=t: e.memset(comb[:, t * 8:t * 8 + E], 0.125),
                                    waits=[rt_prev], inc="dve")
                    continue
                trr = mmgroup(ph, psr[b][:, 0:E],
                              [(txn[b][:, j * 128:(j + 1) * 128],
                                gwT[:, j * E:(j + 1) * E]) for j in range(DT)],
                              waits=[cp_toks[b], rt_prev])
                c0 = t * 8
                if ROUTER_LITE[0] == 1:
                    rt_prev = ph.op("vector",
                                    lambda e, b=b, c0=c0: e.tensor_copy(
                                        comb[:, c0:c0 + E], psr[b][:, 0:E]),
                                    waits=[trr, rt_prev], inc="dve")
                    continue
                a1 = ph.op("vector",
                           lambda e, b=b: e.reduce_max(rsc[:, 0:1], psr[b][:, 0:E], axis=AXX),
                           waits=[trr, rt_prev], inc="dve")
                a2 = ph.op("vector",
                           lambda e, b=b: e.tensor_scalar(rsc[:, 12:12 + E], psr[b][:, 0:E],
                                                          rsc[:, 0:1], None, OP.is_ge),
                           waits=[a1], inc="dve")
                a3 = ph.op("vector",
                           lambda e, b=b: e.scalar_tensor_tensor(rsc[:, 4:4 + E],
                                                                 rsc[:, 12:12 + E], -1e30,
                                                                 psr[b][:, 0:E],
                                                                 OP.mult, OP.add),
                           waits=[a2], inc="dve")
                a4 = ph.op("vector",
                           lambda e: e.reduce_max(rsc[:, 1:2], rsc[:, 4:4 + E], axis=AXX),
                           waits=[a3], inc="dve")
                a5 = ph.op("vector",
                           lambda e: e.tensor_scalar(rsc[:, 20:20 + E], rsc[:, 4:4 + E],
                                                     rsc[:, 1:2], None, OP.is_ge),
                           waits=[a4], inc="dve")
                a6 = ph.op("vector",
                           lambda e: e.tensor_tensor(rsc[:, 2:3], rsc[:, 1:2],
                                                     rsc[:, 0:1], OP.subtract),
                           waits=[a5], inc="dve")
                a7 = ph.op("scalar",
                           lambda e: e.activation(rsc[:, 3:4], rsc[:, 2:3], AF.Exp),
                           waits=[a6], inc="act")
                a8 = ph.op("vector",
                           lambda e: e.tensor_scalar(rsc[:, 2:3], rsc[:, 3:4],
                                                     1.0, None, OP.add),
                           waits=[a7], inc="dve")
                a9 = ph.op("vector",
                           lambda e: e.reciprocal(rsc[:, 2:3], rsc[:, 2:3]),
                           waits=[a8], inc="dve")
                aA = ph.op("vector",
                           lambda e: e.tensor_tensor(rsc[:, 3:4], rsc[:, 3:4],
                                                     rsc[:, 2:3], OP.mult),
                           waits=[a9], inc="dve")
                aB = ph.op("vector",
                           lambda e: e.tensor_scalar(rsc[:, 28:28 + E], rsc[:, 20:20 + E],
                                                     rsc[:, 3:4], None, OP.mult),
                           waits=[aA], inc="dve")
                rt_prev = ph.op("vector",
                                lambda e, c0=c0: e.scalar_tensor_tensor(
                                    comb[:, c0:c0 + E], rsc[:, 12:12 + E],
                                    rsc[:, 2:3], rsc[:, 28:28 + E],
                                    OP.mult, OP.add),
                                waits=[aB], inc="dve")
        return last_xn, rt_prev

    def attn_a(ph, pst, wo_tok, loc):
        delt = loc["delt"]
        delt2 = loc["delt"]
        Aexp = loc["Aexp"]
        Bu = loc["Bu"]
        h32 = loc["h32"]
        h16 = loc["h16"]
        psA = pst.enter_context(nc.psum_tensor(f"psA_{ph.name}", [NST, S], F32))
        psB = psA
        tdel = []
        for c in range(TC):
            tdel.append(mmgroup(ph, psA[0:1, c * 512:(c + 1) * 512],
                                [(wtT[:, j:j + 1], xnsl(j, c * 512, 512))
                                 for j in range(DT)]))
        tsig = ph.op("scalar",
                     lambda e: e.activation(delt[:, :], psA[0:1, :], AF.Sigmoid,
                                            bias=btcol[0:1, 0:1]),
                     waits=tdel, inc="act")
        tdsc = ph.op("vector",
                     lambda e: e.tensor_scalar(delt2[:, :], delt[:, :], 0.99, 0.01,
                                               OP.mult, OP.add),
                     waits=[tsig], inc="dve")
        tbu = []
        for c in range(TC):
            tbu.append(mmgroup(ph, psB[:, c * 512:(c + 1) * 512],
                               [(wbT[:, j * NST:(j + 1) * NST], xnsl(j, c * 512, 512))
                                for j in range(DT)],
                               waits=[tsig, tdsc]))
        tbue = ph.op("scalar",
                     lambda e: e.activation(Bu[:, :], psB[:, :], AF.Identity,
                                            bias=bbcol[:, 0:1]),
                     waits=tbu + [tsig], inc="act")
        tbc = []
        for c in range(TC):
            tbc.append(mmgroup(ph, psA[:, c * 512:(c + 1) * 512],
                               [(ones16[0:1, 0:NST],
                                 delt2[0:1, c * 512:(c + 1) * 512])],
                               waits=[tsig, tdsc, tbue]))
        texp = ph.op("scalar",
                     lambda e: e.activation(Aexp[:, :], psA[:, :], AF.Exp,
                                            scale=Acol[:, 0:1]),
                     waits=tbc + [tbue], inc="act")
        tscan = ph.op("vector",
                      lambda e: e.tensor_tensor_scan(h32[:, :], Aexp[:, :], Bu[:, :],
                                                     0.0, OP.mult, OP.add),
                      waits=[texp, tbue], inc="dve")
        th16 = ph.op("vector", lambda e: e.tensor_copy(h16[:, :], h32[:, :]),
                     waits=[tscan], inc="dve")
        return th16, texp

    def attn_a2(ph, pst, th16, texp, wo_tok, loc):
        h16 = loc["h16"]
        py = [pst.enter_context(nc.psum_tensor(f"py{i}_{ph.name}", [128, 512], F32))
              for i in range(2)]
        po = [pst.enter_context(nc.psum_tensor(f"po{i}_{ph.name}", [128, 512], F32))
              for i in range(2)]
        ye = [None, None]
        last_ye = None
        idx = 0
        for j in range(DT):
            for c in range(TC):
                b = idx % 2
                tmm = mmgroup(ph, py[b][:, :],
                              [(wcT[:, j * 128:(j + 1) * 128],
                                h16[:, c * 512:(c + 1) * 512])],
                              waits=[th16, ye[b], texp],
                              extra=[(brow["bc"][0:1, j * 128:(j + 1) * 128],
                                      onesb[0:1, 0:512])])
                ye[b] = ph.op("vector",
                              lambda e, j=j, c=c, b=b: e.scalar_tensor_tensor(
                                  bigsl(j * S + c * 512, 512),
                                  xnsl(j, c * 512, 512), Dcol[:, j:j + 1],
                                  py[b][:, :], OP.mult, OP.add),
                              waits=[tmm], inc="dve")
                last_ye = ye[b]
                idx += 1
        xe = [None, None]
        idx = 0
        for t in range(TT):
            for c in range(HC):
                b = idx % 2
                tmm = mmgroup(ph, po[b][:, :],
                              [(bigsl(j * S + t * 128, 128),
                                wfull[:, j * D + c * 512: j * D + c * 512 + 512])
                               for j in range(DT)],
                              waits=[last_ye, xe[b], wo_tok],
                              extra=[(onesb[0:1, 0:128],
                                      brow["boa"][0:1, c * 512:(c + 1) * 512])])
                xe[b] = ph.op("vector",
                              lambda e, t=t, c=c, b=b: e.tensor_tensor(
                                  xsl(t, c * 512, 512), xsl(t, c * 512, 512),
                                  po[b][:, :], OP.add),
                              waits=[tmm], inc="dve")
                idx += 1

    def ffn(ph, pst, rt_tok):
        wg = [lb(pst, "wg", 128, DT * 512, BF16) for i in range(2)]
        wu = [lb(pst, "wu", 128, DT * 512, BF16) for i in range(2)]
        wd = [lb(pst, "wd", 128, 4 * D, BF16)]
        gtmp = [lb(pst, "gtmp", 128, 512, BF16) for i in range(2)]
        psG = [pst.enter_context(nc.psum_tensor(f"psG{i}_{ph.name}", [128, 512], F32))
               for i in range(2)]
        psU = [pst.enter_context(nc.psum_tensor(f"psU{i}_{ph.name}", [128, 512], F32))
               for i in range(2)]
        psD = [pst.enter_context(nc.psum_tensor(f"psD{i}_{ph.name}", [128, 512], F32))
               for i in range(2)]
        NWIN = E * NW
        wtoks = []
        consume = [None] * NWIN
        he = {}
        xe = [None, None]

        def issue_w(win):
            e_, w_ = divmod(win, NW)
            wb = win % 2
            wcons = consume[win - 2] if win >= 2 else None
            gsrc = apS(guT_d, e_ * D * 2 * F + w_ * 512,
                       [[2 * F, 128], [128 * 2 * F, DT], [1, 512]])
            usrc = apS(guT_d, e_ * D * 2 * F + F + w_ * 512,
                       [[2 * F, 128], [128 * 2 * F, DT], [1, 512]])
            sm = f"dgp{wb}"
            ph.dma("gpsimd", wg[wb][:, :], gsrc, waits=[wcons], sem=sm)
            t2 = ph.dma("gpsimd", wu[wb][:, :], usrc, waits=[wcons], sem=sm)
            return t2

        def issue_wd(win):
            e_, w_ = divmod(win, NW)
            dsrc = apS(dwT_d, e_ * F * D + w_ * 512 * D,
                       [[D, 128], [128 * D, 4], [1, D]])
            wcons = consume[win - 1] if win >= 1 else None
            return ph.dma("gpsimd", wd[0][:, :], dsrc, waits=[wcons], sem="dwd")

        for win in range(min(2, NWIN)):
            wtoks.append((issue_w(win), issue_wd(win) if win == 0 else None))
        gidx = 0
        didx = 0
        for e_ in range(E):
            for w_ in range(NW):
                win = e_ * NW + w_
                wb = win % 2
                wt, wdt = wtoks[win]
                if wdt is None:
                    wdt = issue_wd(win)
                last_hact = None
                for i in range(4):
                    for c in range(TC):
                        bg = gidx % 2
                        tg_ = mmgroup(ph, psG[bg][:, :],
                                      [(wg[wb][:, j * 512 + i * 128:
                                               j * 512 + i * 128 + 128],
                                        xnsl(j, c * 512, 512)) for j in range(DT)],
                                      waits=[wt, he.get(("g", gidx - 2))])
                        tu_ = mmgroup(ph, psU[bg][:, :],
                                      [(wu[wb][:, j * 512 + i * 128:
                                               j * 512 + i * 128 + 128],
                                        xnsl(j, c * 512, 512)) for j in range(DT)],
                                      waits=[wt, he.get(("h", gidx - 2))])
                        tge = ph.op("scalar",
                                    lambda e, bg=bg: e.activation(
                                        gtmp[bg][:, :], psG[bg][:, :], AF.Gelu),
                                    waits=[tg_, he.get(("h", gidx - 2))],
                                    inc="act")
                        he[("g", gidx)] = tge
                        th = ph.op("vector",
                                   lambda e, i=i, c=c, bg=bg: e.tensor_tensor(
                                       bigsl(i * S + c * 512, 512), psU[bg][:, :],
                                       gtmp[bg][:, :], OP.mult),
                                   waits=[tu_, tge,
                                          consume[win - 1] if win >= 1 else None],
                                   inc="dve")
                        he[("h", gidx)] = th
                        last_hact = th
                        gidx += 1
                last_mm = None
                for t in range(TT):
                    for hcc in range(HC):
                        bd = didx % 2
                        tdn = mmgroup(ph, psD[bd][:, :],
                                      [(bigsl(i * S + t * 128, 128),
                                        wd[0][:, i * D + hcc * 512:
                                              i * D + hcc * 512 + 512])
                                       for i in range(4)],
                                      waits=[last_hact, wdt, xe[bd]])
                        xe[bd] = ph.op("vector",
                                       lambda e, t=t, hcc=hcc, bd=bd, e_=e_:
                                       e.scalar_tensor_tensor(
                                           xsl(t, hcc * 512, 512), psD[bd][:, :],
                                           comb[:, t * 8 + e_:t * 8 + e_ + 1],
                                           xsl(t, hcc * 512, 512),
                                           OP.mult, OP.add),
                                       waits=[tdn, rt_tok], inc="dve")
                        last_mm = tdn
                        didx += 1
                consume[win] = last_mm
                nxt = win + 2
                if nxt < NWIN:
                    wtoks.append((issue_w(nxt), None))

    def attnb_z(ph, pst, last_xn, wz_tok):
        xn32 = [lb(pst, "zn32", 128, D, F32) for i in range(2)]
        txn = [lb(pst, "ztxn", 128, D, F32) for i in range(2)]
        znb = [lb(pst, "znb", 128, D, BF16) for i in range(2)]
        sqdump = lb(pst, "zsqdump", 128, D, F32)
        gbc = lb(pst, "gbc", 128, D, F32)
        bbc = lb(pst, "bbc", 128, D, F32)
        browf = {nm: lb(pst, "zrow_" + nm, 1, D, F32) for nm in ("lng", "lnb")}
        psZ = [pst.enter_context(nc.psum_tensor(f"psZ{i}_{ph.name}", [128, D], F32))
               for i in range(2)]
        psbz = pst.enter_context(nc.psum_tensor(f"psbz_{ph.name}", [128, 512], F32))
        tgb = None
        trow = {nm: ph.dma("sync", browf[nm][:, :],
                           apS(brow_d[nm], 0, [[D, 1], [1, D]]))
                for nm in ("lng", "lnb")}
        for nm, dst in (("lng", gbc), ("lnb", bbc)):
            for c in range(HC):
                tmm = mmgroup(ph, psbz[:, :],
                              [(onesf[0:1, 0:128],
                                browf[nm][0:1, c * 512:(c + 1) * 512])],
                              waits=[tgb, trow["lnb"]])
                tgb = ph.op("vector",
                            lambda e, dst=dst, c=c: e.tensor_copy(
                                dst[:, c * 512:(c + 1) * 512], psbz[:, :]),
                            waits=[tmm], inc="dve")
        rd = [[], []]
        tsq = None
        tsum = None
        for t in range(TT):
            b = t % 2
            grp = []
            for hc in range(HC):
                grp.append(mmgroup(ph, psZ[b][:, hc * 512:hc * 512 + 512],
                                   [(xnsl(j, t * 128, 128),
                                     wfull[:, j * D + hc * 512: j * D + hc * 512 + 512])
                                    for j in range(DT)],
                                   waits=[last_xn, wz_tok] + rd[b],
                                   extra=[(onesb[0:1, 0:128],
                                           brow["bz"][0:1, hc * 512:hc * 512 + 512])]))
            tsum = ph.op("vector",
                         lambda e, t=t, b=b: e.reduce_sum(zstats[:, t:t + 1],
                                                          psZ[b][:, :], axis=AXX),
                         waits=grp, inc="dve")
            tcp = ph.op("vector",
                        lambda e, t=t, b=b: e.tensor_copy(bigsl(t * D, D),
                                                          psZ[b][:, :]),
                        waits=grp, inc="dve")
            tsq = ph.op("scalar",
                        lambda e, t=t, b=b: e.activation(
                            sqdump[:, :], psZ[b][:, :], AF.Square,
                            accum_out=zstats[:, TT + t:TT + t + 1]),
                        waits=grp + [tsq, tcp], inc="act")
            rd[b] = [tsum, tsq, tcp]
        t1 = ph.op("vector",
                   lambda e: e.tensor_scalar_mul(zstats[:, 2 * TT:3 * TT],
                                                 zstats[:, 0:TT], 1.0 / D),
                   waits=[tsum], inc="dve")
        t2 = ph.op("vector",
                   lambda e: e.tensor_tensor(zstats[:, 3 * TT:4 * TT],
                                             zstats[:, 2 * TT:3 * TT],
                                             zstats[:, 2 * TT:3 * TT], OP.mult),
                   waits=[t1], inc="dve")
        t3 = ph.op("vector",
                   lambda e: e.scalar_tensor_tensor(zstats[:, 3 * TT:4 * TT],
                                                    zstats[:, TT:2 * TT], 1.0 / D,
                                                    zstats[:, 3 * TT:4 * TT],
                                                    OP.mult, OP.subtract),
                   waits=[t2, tsq], inc="dve")
        t4 = ph.op("scalar",
                   lambda e: e.activation(zstats[:, 3 * TT:4 * TT],
                                          zstats[:, 3 * TT:4 * TT],
                                          AF.Sqrt, bias=eps5[:, 0:1]),
                   waits=[t3], inc="act")
        t5 = ph.op("vector",
                   lambda e: e.reciprocal(zstats[:, 3 * TT:4 * TT],
                                          zstats[:, 3 * TT:4 * TT]),
                   waits=[t4], inc="dve")
        psT = [pst.enter_context(nc.psum_tensor(f"psTz{i}_{ph.name}", [128, D], BF16))
               for i in range(2)]
        tr_toks = [None, None]
        ev_toks = [None, None]
        tz3p = [None, None]
        last_zn = None
        for t in range(TT):
            b = t % 2
            tz1 = ph.op("vector",
                        lambda e, t=t, b=b: e.tensor_scalar(
                            xn32[b][:, :], bigsl(t * D, D),
                            zstats[:, 2 * TT + t:2 * TT + t + 1],
                            zstats[:, 3 * TT + t:3 * TT + t + 1],
                            OP.subtract, OP.mult),
                        waits=[t5, tr_toks[b]], inc="dve")
            tz2 = ph.op("vector",
                        lambda e, b=b: e.tensor_tensor(txn[b][:, :], xn32[b][:, :],
                                                       gbc[:, :], OP.mult),
                        waits=[tz1, tz3p[b]], inc="dve")
            tz3 = ph.op("vector",
                        lambda e, b=b: e.tensor_tensor(znb[b][:, :], txn[b][:, :],
                                                       bbc[:, :], OP.add),
                        waits=[tz2, tr_toks[b]], inc="dve")
            tz3p[b] = tz3

            def trf(e, b=b):
                last = None
                for j in range(DT):
                    last = e.transpose(psT[b][:, j * 128:(j + 1) * 128],
                                       znb[b][:, j * 128:(j + 1) * 128],
                                       idbf[:, :])
                return last

            ttr = ph.op("tensor", trf, waits=[tz3, ev_toks[b]], inc="pe")
            tr_toks[b] = ttr
            ev_toks[b] = ph.op("vector",
                               lambda e, t=t, b=b: e.tensor_copy(
                                   apS(xnT, t * 128,
                                       [[DT * S, 128], [S, DT], [1, 128]]),
                                   apS(psT[b], 0,
                                       [[D, 128], [128, DT], [1, 128]])),
                               waits=[ttr], inc="dve")
            last_zn = ev_toks[b]
        return last_zn

    def attnb_qrs(ph, pst, last_zn):
        wqs = [lb(pst, "wqs", 128, DT * 128, BF16)]
        wrs = [lb(pst, "wrs", 128, DT * 128, BF16)]
        whs = [lb(pst, "whs", 128, DT * 128, BF16)]
        QRb = [lb(pst, "QRb", 128, S, BF16)]
        Qc = [lb(pst, "Qc", 128, S, BF16) for i in range(2)]
        Rc = [lb(pst, "Rc", 128, S, BF16) for i in range(2)]
        RcN = lb(pst, "RcN", 128, S, BF16)
        hv = [lb(pst, "hv", 128, S, BF16) for i in range(2)]
        mrg = lb(pst, "mrg", 128, 4224, BF16)
        scb = [mrg[:, 0:512], mrg[:, 512:1024]]
        tphi = [mrg[:, 1024:1536], mrg[:, 1536:2048]]
        tmid = [mrg[:, 2048:2560], mrg[:, 2560:3072]]
        tsin = [mrg[:, 3072:3584], mrg[:, 3584:4096]]
        Gsb = mrg[:, 4096:4224]
        prod = mrg[:, 0:512]
        nrmrow = mrg[0:1, 512:1024]
        rcols = lb(pst, "rcols", 128, TT, F32)
        psA = [pst.enter_context(nc.psum_tensor(f"psA{i}_{ph.name}", [128, 512], F32))
               for i in range(2)]
        psS = [pst.enter_context(nc.psum_tensor(f"psS{i}_{ph.name}", [128, 512], F32))
               for i in range(2)]
        psO = pst.enter_context(nc.psum_tensor(f"psO_{ph.name}", [128, TT * 64], F32))
        psNG = pst.enter_context(nc.psum_tensor(f"psNG_{ph.name}", [128, 512], F32))
        psTb = pst.enter_context(nc.psum_tensor(f"psTb_{ph.name}", [128, 512], BF16))

        NH = D // 64
        psA_ev = [None, None]
        wq_cons = [None]
        wr_cons = [None]
        wh_cons = [None]
        wq_tok = [None] * NH
        wr_tok = [None] * NH
        wh_tok = [None] * (NH // 2)
        qrb_free = [[]]
        qcrc_free = [None, None]
        hv_free = [None, None]
        whs_used = [None]
        psTb_ev = [None]
        psNG_free = [None]
        gsb_used = [None]
        prod_used = [None]
        nrm_used = [None]
        rcols_used = [None]
        oev_last = [None]
        tphi_rd = [None, None]
        tmid_rd = [None, None]
        tsin_rd = [None, None]
        trig_last = [None, None]

        def issue_qr(h):
            tq = tr = None
            for half in range(2):
                tq = ph.dma("gpsimd",
                            apS(wqs[0], half * 64, [[DT * 128, 128], [128, DT], [1, 64]]),
                            apS(wq_d, h * 64, [[D, 128], [128 * D, DT], [1, 64]]),
                            waits=[wq_cons[0]], sem="dgp0")
                tr = ph.dma("gpsimd",
                            apS(wrs[0], half * 64, [[DT * 128, 128], [128, DT], [1, 64]]),
                            apS(wr_d, h * 64, [[D, 128], [128 * D, DT], [1, 64]]),
                            waits=[wr_cons[0]], sem="dgp0")
            wq_tok[h] = tr
            wr_tok[h] = tr

        def issue_h(hp):
            wh_tok[hp] = ph.dma("gpsimd", whs[0][:, :],
                                apS(wh_d, hp * 128, [[D, 128], [128 * D, DT], [1, 128]]),
                                waits=[whs_used[0]], sem="dgp1")

        def stage1(h):
            b = h % 2
            if h % 2 == 0:
                hp = h // 2
                hb = hp % 2
                for t4 in range(TT // 4):
                    par = t4 % 2

                    def hvf(e, t4=t4, hb=hb, hp=hp, par=par):
                        last = None
                        for tl in range(4):
                            t = t4 * 4 + tl
                            for j in range(DT):
                                last = e.matmul(psA[par][:, tl * 128:(tl + 1) * 128],
                                                xnsl(j, t * 128, 128),
                                                whs[0][:, j * 128:(j + 1) * 128],
                                                start=(j == 0), stop=False)
                            last = e.matmul(psA[par][:, tl * 128:(tl + 1) * 128],
                                            onesb[0:1, 0:128],
                                            brow["bh"][0:1, hp * 128:hp * 128 + 128],
                                            start=False, stop=True)
                        return last

                    tmm = ph.op("tensor", hvf,
                                waits=[wh_tok[hp], last_zn, psA_ev[par], hv_free[hb]],
                                inc="pe")
                    whs_used[0] = tmm
                    psA_ev[par] = ph.op("vector",
                                        lambda e, t4=t4, hb=hb, par=par: e.tensor_copy(
                                            hv[hb][:, t4 * 512:(t4 + 1) * 512],
                                            psA[par][:, :]),
                                        waits=[tmm], inc="dve")
                if hp + 1 < NH // 2:
                    issue_h(hp + 1)
            for (wsb, wtk, bcol, is_q) in ((wqs, wq_tok[h], bqdup, True),
                                           (wrs, wr_tok[h], brdup, False)):
                evs = []
                for c in range(TC):
                    par = c % 2
                    tmm = mmgroup(ph, psA[par][:, :],
                                  [(wsb[0][:, j * 128:(j + 1) * 128],
                                    xnsl(j, c * 512, 512)) for j in range(DT)],
                                  waits=[wtk, last_zn, psA_ev[par]])
                    if is_q:
                        wq_cons[0] = tmm
                    else:
                        wr_cons[0] = tmm
                    ev = ph.op("scalar",
                               lambda e, c=c, par=par, bcol=bcol, h=h: e.activation(
                                   QRb[0][:, c * 512:(c + 1) * 512], psA[par][:, :],
                                   AF.Identity, bias=bcol[:, h:h + 1]),
                               waits=[tmm] + qrb_free[0], inc="act")
                    psA_ev[par] = ev
                    evs.append(ev)
                qrb_free[0] = []
                dstC = Qc[b] if is_q else Rc[b]
                t5s = []
                for c in range(TC):
                    par = c % 2
                    p = c % 2
                    tmm = mmgroup(ph, psA[par][:, :],
                                  [(wphi2[:, :], QRb[0][0:64, c * 512:(c + 1) * 512])],
                                  waits=evs + [psA_ev[par]])
                    qrb_free[0].append(tmm)
                    t1 = ph.op("scalar",
                               lambda e, p=p, par=par: e.activation(
                                   tphi[p], psA[par][:, :], AF.Identity,
                                   bias=biascol[:, 0:1]),
                               waits=[tmm, tphi_rd[p]], inc="act")
                    psA_ev[par] = t1
                    t2 = ph.op("vector",
                               lambda e, p=p: e.tensor_scalar(
                                   tmid[p], tphi[p], PI, None, OP.is_gt),
                               waits=[t1, tmid_rd[p]], inc="dve")
                    t3 = ph.op("vector",
                               lambda e, p=p: e.scalar_tensor_tensor(
                                   tmid[p], tmid[p], -2.0 * PI,
                                   tphi[p], OP.mult, OP.add),
                               waits=[t2], inc="dve")
                    tphi_rd[p] = t3
                    t4 = ph.op("scalar",
                               lambda e, p=p: e.activation(
                                   tsin[p], tmid[p], AF.Sin),
                               waits=[t3, tsin_rd[p]], inc="act")
                    tmid_rd[p] = t4
                    t5 = ph.op("vector",
                               lambda e, p=p, c=c, dstC=dstC: e.tensor_tensor(
                                   dstC[:, c * 512:(c + 1) * 512],
                                   QRb[0][:, c * 512:(c + 1) * 512],
                                   tsin[p], OP.mult),
                               waits=[t4, qcrc_free[b]], inc="dve")
                    tsin_rd[p] = t5
                    qrb_free[0].append(t5)
                    t5s.append(t5)
                if is_q:
                    trig_last[0] = t5s[-1]
                else:
                    trig_last[1] = t5s[-1]
            if h + 1 < NH:
                issue_qr(h + 1)
            return trig_last[0], trig_last[1]

        def stage3(h, tq_done, tr_done):
            b = h % 2
            tr_evs = []
            for g in range(TKT // 4):

                def trf(e, g=g, b=b):
                    last = None
                    for i in range(4):
                        kt = g * 4 + i
                        last = e.transpose(psTb[:, i * 128:(i + 1) * 128],
                                           Rc[b][:, kt * 128:(kt + 1) * 128],
                                           idbf[:, :])
                    return last

                ttr = ph.op("tensor", trf, waits=[tr_done, psTb_ev[0]], inc="pe")
                tev = ph.op("vector",
                            lambda e, g=g: e.tensor_copy(
                                RcN[:, g * 512:(g + 1) * 512], psTb[:, :]),
                            waits=[ttr], inc="dve")
                psTb_ev[0] = tev
                tr_evs.append(tev)

            def gf(e):
                last = None
                for kt in range(TKT):
                    last = e.matmul(psNG[:, 0:128],
                                    RcN[:, kt * 128:(kt + 1) * 128],
                                    RcN[:, kt * 128:(kt + 1) * 128],
                                    start=(kt == 0), stop=(kt == TKT - 1))
                return last

            tg = ph.op("tensor", gf, waits=tr_evs + [psNG_free[0]], inc="pe")
            tgev = ph.op("vector", lambda e: e.tensor_copy(Gsb, psNG[:, 0:128]),
                         waits=[tg, gsb_used[0]], inc="dve")
            tn_prev = None
            ttp = None
            for qc in range(TC):
                par = qc % 2
                tq_ = mmgroup(ph, psA[par][:, :],
                              [(Gsb, Qc[b][:, qc * 512:(qc + 1) * 512])],
                              waits=[tgev, tq_done, psA_ev[par]])
                gsb_used[0] = tq_
                tp = ph.op("vector",
                           lambda e, par=par, qc=qc, b=b: e.tensor_tensor(
                               prod, psA[par][:, :],
                               Qc[b][:, qc * 512:(qc + 1) * 512], OP.mult),
                           waits=[tq_, prod_used[0]], inc="dve")
                psA_ev[par] = tp

                def nf(e):
                    return e.matmul(psNG[0:1, 0:512], onescol[:, 0:1], prod,
                                    start=True, stop=True)

                tn = ph.op("tensor", nf, waits=[tp, tgev, tn_prev], inc="pe")
                prod_used[0] = tn
                tc_ = ph.op("vector",
                            lambda e: e.tensor_copy(nrmrow, psNG[0:1, 0:512]),
                            waits=[tn, nrm_used[0]], inc="dve")
                tn_prev = tc_
                tf1 = ph.op("scalar", lambda e: e.activation(nrmrow, nrmrow, AF.Sqrt),
                            waits=[tc_], inc="act")
                tf2 = ph.op("vector", lambda e: e.tensor_scalar(nrmrow, nrmrow,
                                                                1e-12, None, OP.max),
                            waits=[tf1], inc="dve")

                def trp(e, qc=qc):
                    last = None
                    for i in range(4):
                        c0 = (qc * 4 + i) * 2
                        last = e.transpose(psTb[:, c0:c0 + 1],
                                           nrmrow[0:1, i * 128:(i + 1) * 128],
                                           idbf[0:1, 0:1])
                    return last

                ttp = ph.op("tensor", trp, waits=[tf2, psTb_ev[0]], inc="pe")
                nrm_used[0] = ttp
            trc0 = ph.op("vector", lambda e: e.tensor_copy(
                             rcols[:, 0:TT], apS(psTb, 0, [[512, 128], [2, TT]])),
                         waits=[ttp, rcols_used[0]], inc="dve")
            psTb_ev[0] = trc0
            trc = ph.op("vector", lambda e: e.reciprocal(rcols[:, 0:TT],
                                                         rcols[:, 0:TT]),
                        waits=[trc0], inc="dve")
            psNG_free[0] = tn_prev
            hb = (h // 2) % 2
            sc_cp = [None, None]
            prev_out = [None, None]
            last_out = None
            kidx = 0
            for qc in range(TC):
                for kt in range(TKT):
                    par = kidx % 2
                    tmm = mmgroup(ph, psS[par][:, :],
                                  [(Rc[b][:, kt * 128:(kt + 1) * 128],
                                    Qc[b][:, qc * 512:(qc + 1) * 512])],
                                  waits=[tq_done, tr_done, sc_cp[par]])
                    if kidx % 2 == 0:
                        tcp = ph.op("vector",
                                    lambda e, par=par: e.tensor_copy(
                                        scb[par], psS[par][:, :]),
                                    waits=[tmm, prev_out[par]], inc="dve")
                    else:
                        tcp = ph.op("scalar",
                                    lambda e, par=par: e.activation(
                                        scb[par], psS[par][:, :], AF.Identity),
                                    waits=[tmm, prev_out[par]], inc="act")
                    sc_cp[par] = tcp

                    def outf(e, kt=kt, qc=qc, par=par, h=h, hb=hb):
                        last = None
                        for qtl in range(4):
                            qt = qc * 4 + qtl
                            stt_ = (kt == 0 and qt % 8 == 0)
                            stp_ = (kt == TKT - 1 and (qt % 8 == 7 or qt == TT - 1))
                            last = e.matmul(
                                psO[:, qt * 64:(qt + 1) * 64],
                                mrg[:, par * 512 + qtl * 128:
                                    par * 512 + (qtl + 1) * 128],
                                hv[hb][:, kt * 128 + (h % 2) * 64:
                                       kt * 128 + (h % 2) * 64 + 64],
                                start=stt_, stop=stp_)
                        return last

                    tout = ph.op("tensor", outf, waits=[tcp, oev_last[0]], inc="pe")
                    prev_out[par] = tout
                    last_out = tout
                    kidx += 1
            qcrc_free[b] = last_out
            hv_free[hb] = last_out
            oev = None
            for qt in range(TT):
                oev = ph.op("scalar",
                            lambda e, qt=qt, h=h: e.activation(
                                bigsl(qt * D + h * 64, 64),
                                psO[:, qt * 64:(qt + 1) * 64], AF.Identity,
                                scale=rcols[:, qt:qt + 1]),
                            waits=[trc, last_out, last_zn], inc="act")
            oev_last[0] = oev
            rcols_used[0] = oev

        issue_qr(0)
        issue_h(0)
        pend = None
        for h in range(NH):
            tqd, trd = stage1(h)
            if pend is not None:
                stage3(*pend)
            pend = (h, tqd, trd)
        stage3(*pend)
        return [oev_last[0]]


    def attnb_wo(ph, pst, ot_toks, wfull_prev):
        wo_tok = ph.dma("gpsimd", wfull[:, :],
                        apS(wob_d, 0, [[D, 128], [128 * D, DT], [1, D]]),
                        waits=[wfull_prev[0]], sem="dwf")
        wfull_prev[0] = wo_tok
        psT = [pst.enter_context(nc.psum_tensor(f"psTo{i}_{ph.name}", [128, D], BF16))
               for i in range(2)]
        psF = [pst.enter_context(nc.psum_tensor(f"psF{i}_{ph.name}", [128, 512], F32))
               for i in range(2)]
        ev = [None, None]
        last_ev = None
        for qt in range(TT):
            b = qt % 2

            def trf(e, qt=qt, b=b):
                last = None
                for j in range(DT):
                    last = e.transpose(psT[b][:, j * 128:(j + 1) * 128],
                                       bigsl(qt * D + j * 128, 128), idbf[:, :])
                return last

            ttr = ph.op("tensor", trf, waits=ot_toks + [ev[b]], inc="pe")
            ev[b] = ph.op("vector",
                          lambda e, qt=qt, b=b: e.tensor_copy(
                              apS(xnT, qt * 128, [[DT * S, 128], [S, DT], [1, 128]]),
                              apS(psT[b], 0, [[D, 128], [128, DT], [1, 128]])),
                          waits=[ttr], inc="dve")
            last_ev = ev[b]
        xe = [None, None]
        idx = 0
        for t in range(TT):
            for c in range(HC):
                b = idx % 2
                tmm = mmgroup(ph, psF[b][:, :],
                              [(xnsl(j, t * 128, 128),
                                wfull[:, j * D + c * 512: j * D + c * 512 + 512])
                               for j in range(DT)],
                              waits=[last_ev, xe[b], wo_tok],
                              extra=[(onesb[0:1, 0:128],
                                      brow["bob"][0:1, c * 512:(c + 1) * 512])])
                xe[b] = ph.op("vector",
                              lambda e, t=t, c=c, b=b: e.tensor_tensor(
                                  xsl(t, c * 512, 512), xsl(t, c * 512, 512),
                                  psF[b][:, :], OP.add),
                              waits=[tmm], inc="dve")
                idx += 1

    store_toks = []
    prev_store = None
    wfull_prev = [None]
    for batch in range(NB):
        ph = new_ph(f"b{batch}_load")
        src = apS(x_d, batch * S * D, [[D, 128], [128 * D, TT], [1, D]])
        dst = apS(x_sb, 0, [[TT * D, 128], [D, TT], [1, D]])
        xl = ph.dma("sync", dst, src, waits=[prev_store])

        ph = new_ph(f"b{batch}_ln1")
        with contextlib.ExitStack() as pst:
            woa_tok = ph.dma("gpsimd", wfull[:, :],
                             apS(woTa_d, 0, [[D, 128], [128 * D, DT], [1, D]]),
                             waits=[wfull_prev[0]], sem="dwf")
            wfull_prev[0] = woa_tok
            ln_phase(ph, pst, router=False, wait_tokens=[xl])

        if "attn_a" in enable:
            ph = new_ph(f"b{batch}_attna")
            with contextlib.ExitStack() as loc_st:
                loc = {
                    "delt": lb(loc_st, "delt", 1, S, F32),
                    "Aexp": lb(loc_st, "Aexp", NST, S, F32),
                    "Bu": lb(loc_st, "Bu", NST, S, F32),
                    "h32": lb(loc_st, "h32", NST, S, F32),
                    "h16": lb(loc_st, "h16", NST, S, BF16),
                }
                with contextlib.ExitStack() as pst:
                    th16, texp = attn_a(ph, pst, woa_tok, loc)
                with contextlib.ExitStack() as pst:
                    attn_a2(ph, pst, th16, texp, woa_tok, loc)

        if "ffn1" in enable or "router" in enable:
            ph = new_ph(f"b{batch}_ln2")
            with contextlib.ExitStack() as pst:
                _, rt = ln_phase(ph, pst, router=True)
            if "ffn1" in enable:
                ph = new_ph(f"b{batch}_ffn1")
                with contextlib.ExitStack() as pst:
                    ffn(ph, pst, rt)

        if "attn_b" in enable:
            ph = new_ph(f"b{batch}_ln3")
            with contextlib.ExitStack() as pst:
                wz_tok = ph.dma("gpsimd", wfull[:, :],
                                apS(wz_d, 0, [[D, 128], [128 * D, DT], [1, D]]),
                                waits=[wfull_prev[0]], sem="dwf")
                wfull_prev[0] = wz_tok
                last_xn, _ = ln_phase(ph, pst, router=False)
            ph = new_ph(f"b{batch}_z")
            with contextlib.ExitStack() as pst:
                last_zn = attnb_z(ph, pst, last_xn, wz_tok)
            ph = new_ph(f"b{batch}_qrs")
            with contextlib.ExitStack() as pst:
                ot_toks = attnb_qrs(ph, pst, last_zn)
            ph = new_ph(f"b{batch}_wo")
            with contextlib.ExitStack() as pst:
                attnb_wo(ph, pst, ot_toks, wfull_prev)

        if "ffn2" in enable:
            ph = new_ph(f"b{batch}_ln4")
            with contextlib.ExitStack() as pst:
                _, rt = ln_phase(ph, pst, router=True)
            ph = new_ph(f"b{batch}_ffn2")
            with contextlib.ExitStack() as pst:
                ffn(ph, pst, rt)

        ph = new_ph(f"b{batch}_store")
        dst = apS(out_d, batch * S * D, [[D, 128], [128 * D, TT], [1, D]])
        src = apS(x_sb, 0, [[TT * D, 128], [D, TT], [1, D]])
        prev_store = ph.dma("sync", dst, src)
        store_toks.append(prev_store)

    ph = new_ph("final")
    ph.op("sync", lambda e: e.nop(), waits=store_toks)

    for ph_ in phases:
        ph_.emit()
    k.close()
    st.close()
    nc.finalize()
    return nc


def host_prep(cfg, inputs):
    DT = cfg["D"] // 128
    f32 = np.float32
    bf = ml_dtypes.bfloat16
    g = lambda n: np.asarray(inputs[n], dtype=f32)
    shared = {
        "guT": np.ascontiguousarray(g("L0_ffn_gate_up_w").transpose(0, 2, 1)),
        "dwT": np.ascontiguousarray(g("L0_ffn_down_w").transpose(0, 2, 1)),
        "gwT": np.ascontiguousarray(g("L0_ffn_gate_w").T),
        "woTa": np.ascontiguousarray(g("L0_attn_a_W_o").T),
        "wbT": np.ascontiguousarray(g("L0_attn_a_W_b").T),
        "wtT": np.ascontiguousarray(g("L0_attn_a_W_t").reshape(-1, 128)),
        "wcT": np.ascontiguousarray(g("L0_attn_a_W_c").T),
        "A": g("L0_attn_a_A"),
        "bb": g("L0_attn_a_b_b"),
        "bt": g("L0_attn_a_b_t"),
        "b_bc": g("L0_attn_a_b_c"),
        "b_boa": g("L0_attn_a_b_o"),
        "Dg": np.ascontiguousarray(g("L0_attn_a_D").reshape(DT, 128)),
        "wz": g("L0_attn_b_Wz"),
        "wq": g("L0_attn_b_Wq"),
        "wr": g("L0_attn_b_Wr"),
        "wh": g("L0_attn_b_Wh"),
        "wob": g("L0_attn_b_Wo"),
        "b_bz": g("L0_attn_b_bz"),
        "bq": np.ascontiguousarray(np.vstack([g("L0_attn_b_bq").reshape(16, 64).T] * 2)),
        "br": np.ascontiguousarray(np.vstack([g("L0_attn_b_br").reshape(16, 64).T] * 2)),
        "b_bh": g("L0_attn_b_bh"),
        "b_bob": g("L0_attn_b_bo"),
        "b_lng": g("L0_attn_b_ln_g"),
        "b_lnb": g("L0_attn_b_ln_b"),
        "wphi": np.ascontiguousarray(np.concatenate([g("L0_attn_b_Wphi")] * 2, axis=1)),
        "bphi": np.ascontiguousarray(np.concatenate([g("L0_attn_b_bphi")] * 2).reshape(128, 1)),
        "id32": np.eye(128, dtype=f32),
        "idbf": np.eye(128, dtype=bf),
        "ones_f": np.ones((1, 128), dtype=f32),
        "ones_bf": np.ones((1, 512), dtype=bf),
        "ones16": np.ones((1, 16), dtype=f32),
    }
    return shared


def kernel(**inputs):
    cfg = CFG_FULL
    NB = cfg["BATCHES"]
    x = np.asarray(inputs["L0_attn_a_x"], dtype=np.float32)
    n_cores = 8
    shared = host_prep(cfg, inputs)
    nc = build(cfg)
    in_maps = []
    for c in range(n_cores):
        m = dict(shared)
        m["x"] = np.ascontiguousarray(x[c * NB:(c + 1) * NB])
        in_maps.append(m)
    res = run_bass_kernel_spmd(nc, in_maps, core_ids=list(range(n_cores)))
    outs = [np.asarray(r["out"]) for r in res.results]
    return np.concatenate(outs, axis=0).astype(np.float32)


# version 07
# speedup vs baseline: 1.0793x; 1.0085x over previous
import contextlib
import numpy as np
import ml_dtypes
import concourse.bass as bass
import concourse.mybir as mybir
from concourse.bass_utils import run_bass_kernel_spmd
from concourse.alu_op_type import AluOpType as OP

F32 = mybir.dt.float32
BF16 = mybir.dt.bfloat16
AF = mybir.ActivationFunctionType
AXX = mybir.AxisListType.X
ENGINES = ("tensor", "vector", "scalar", "gpsimd", "sync")
PI = float(np.pi)

CFG_FULL = dict(S=2048, D=1024, NST=16, F=4096, E=8, BATCHES=2)
ROUTER_LITE = [0]
ALL_PARTS = ("attn_a", "ffn1", "attn_b", "ffn2")


class K:
    def __init__(self, nc):
        self.nc = nc
        self.sems = {}
        self.shadow = {}
        self.stack = contextlib.ExitStack()
        for name in ("pe", "dve", "act", "pool", "dsync", "dgp", "dact",
                     "dgp0", "dgp1", "dwf", "dwd", "dinit"):
            self.sems[name] = self.stack.enter_context(nc.semaphore(name))
            self.shadow[name] = 0
        self.compute_sems = ("pe", "dve", "act", "pool")

    def close(self):
        self.stack.close()


class Ph:

    def __init__(self, k, name=""):
        self.k = k
        self.name = name
        self.ops = {e: [] for e in ENGINES}
        self.base = {s: k.shadow[s] for s in k.compute_sems}

    def op(self, eng, fn, waits=(), inc=None, n=1):
        tok = None
        if inc is not None:
            self.k.shadow[inc] += n
            tok = (inc, self.k.shadow[inc])
        wl = []
        for w in waits:
            if w is None:
                continue
            if isinstance(w, list):
                wl.extend(x for x in w if x is not None)
            else:
                wl.append(w)
        self.ops[eng].append((fn, wl, inc, n))
        return tok

    def dma(self, eng, dst, src, waits=(), sem=None):
        if sem is None:
            sem = {"sync": "dsync", "gpsimd": "dgp", "scalar": "dact"}[eng]
        return self.op(eng, lambda e, d=dst, s=src: e.dma_start(d, s),
                       waits=waits, inc=sem, n=16)

    def fence(self):
        return [("dve", self.k.shadow["dve"]), ("act", self.k.shadow["act"])]

    def emit(self):
        k = self.k
        with k.nc.Block(self.name or None) as block:
            for eng in ENGINES:
                if not self.ops[eng]:
                    continue

                def body(e, ops=self.ops[eng], base=dict(self.base)):
                    hw = base
                    for fn, waits, inc, n in ops:
                        agg = {}
                        for (sem, val) in waits:
                            if val > agg.get(sem, 0):
                                agg[sem] = val
                        for sem, val in agg.items():
                            if hw.get(sem, 0) >= val:
                                continue
                            e.wait_ge(k.sems[sem], val)
                            hw[sem] = val
                        instr = fn(e)
                        if inc is not None:
                            instr.then_inc(k.sems[inc], n)

                getattr(block, eng)(body)


def build(cfg, enable=ALL_PARTS):
    S, D, NST, F, E, NB = (cfg["S"], cfg["D"], cfg["NST"], cfg["F"],
                           cfg["E"], cfg["BATCHES"])
    TT, DT, TC, HC = S // 128, D // 128, S // 512, D // 512
    NW = F // 512
    NHP = D // 128
    TKT = TT
    assert S % 512 == 0 and D % 512 == 0 and F % 512 == 0 and DT >= 4

    nc = bass.Bass("TRN2", target_bir_lowering=False)
    k = K(nc)

    def din(name, shape, dt=F32):
        return nc.dram_tensor(name, list(shape), dt, kind="ExternalInput")

    x_d = din("x", (NB, S, D))
    out_d = nc.dram_tensor("out", [NB, S, D], F32, kind="ExternalOutput")
    guT_d = din("guT", (E, D, 2 * F))
    dwT_d = din("dwT", (E, F, D))
    gwT_d = din("gwT", (D, E))
    woTa_d = din("woTa", (D, D))
    wbT_d = din("wbT", (D, NST))
    wtT_d = din("wtT", (DT, 128))
    wcT_d = din("wcT", (NST, D))
    A_d = din("A", (NST,))
    bb_d = din("bb", (NST,))
    bt_d = din("bt", (1,))
    Dg_d = din("Dg", (DT, 128))
    wz_d = din("wz", (D, D))
    wq_d = din("wq", (D, D))
    wr_d = din("wr", (D, D))
    wh_d = din("wh", (D, D))
    wob_d = din("wob", (D, D))
    bq_d = din("bq", (128, 16))
    br_d = din("br", (128, 16))
    wphi_d = din("wphi", (64, 128))
    bphi_d = din("bphi", (128, 1))
    id32_d = din("id32", (128, 128))
    idbf_d = din("idbf", (128, 128), BF16)
    ones_d = din("ones_f", (1, 128))
    onesbf_d = din("ones_bf", (1, 512), BF16)
    ones16_d = din("ones16", (1, 16))
    brow_d = {nm: din("b_" + nm, (D,))
              for nm in ("bc", "boa", "bz", "bh", "bob", "lng", "lnb")}

    st = contextlib.ExitStack()
    sb = lambda name, p, f, dt: st.enter_context(nc.sbuf_tensor(name, [p, f], dt))
    _uid = [0]

    def lb(pst, name, p, f, dt):
        _uid[0] += 1
        return pst.enter_context(nc.sbuf_tensor(f"{name}_{_uid[0]}", [p, f], dt))

    x_sb = sb("x_sb", 128, TT * D, F32)
    xnT = sb("xnT", 128, DT * S, BF16)
    big = sb("big", 128, TT * D, BF16)
    wfull = sb("wfull", 128, DT * D, BF16)
    stats = sb("stats", 128, 4 * TT, F32)
    zstats = sb("zstats", 128, 4 * TT, F32)
    rsc = sb("rsc", 128, 40, F32)
    comb = sb("comb", 128, TT * 8, F32)
    id32 = sb("id32s", 128, 128, F32)
    idbf = sb("idbfs", 128, 128, BF16)
    onesf = sb("onesf", 1, 128, F32)
    onesb = sb("onesb", 1, 512, BF16)
    ones16 = sb("ones16s", 1, 16, F32)
    onescol = sb("onescol", 128, 1, BF16)
    eps5 = sb("eps5", 128, 1, F32)
    Acol = sb("Acol", NST, 1, F32)
    bbcol = sb("bbcol", NST, 1, F32)
    btcol = sb("btcol", 1, 1, F32)
    biascol = sb("biascol", 128, 1, F32)
    threscol = sb("threscol", 128, 1, F32)
    Dcol = sb("Dcol", 128, DT, F32)
    bqdup = sb("bqdup", 128, 16, F32)
    brdup = sb("brdup", 128, 16, F32)
    rows8 = sb("rows8", DT, 128, F32)
    wbT = sb("wbT_s", 128, DT * NST, BF16)
    wtT = sb("wtT_s", 128, DT, BF16)
    wcT = sb("wcT_s", NST, D, BF16)
    gwT = sb("gwT_s", 128, DT * E, F32)
    wphi2 = sb("wphi2_s", 64, 128, BF16)
    brow = {nm: sb(f"brow_{nm}", 1, D, BF16)
            for nm in ("bc", "boa", "bz", "bh", "bob")}

    phases = []

    def new_ph(name):
        ph = Ph(k, name)
        phases.append(ph)
        return ph

    def apS(t, off, dims):
        return bass.AP(t, off, dims)

    def xsl(t, lo=0, wdt=None):
        wdt = D if wdt is None else wdt
        return x_sb[:, t * D + lo: t * D + lo + wdt]

    def xnsl(j, lo, wdt):
        return xnT[:, j * S + lo: j * S + lo + wdt]

    def bigsl(lo, wdt):
        return big[:, lo:lo + wdt]

    def mmgroup(ph, out_ap, pieces, waits=(), extra=()):
        allp = list(pieces) + list(extra)

        def fn(e, allp=allp, out_ap=out_ap):
            n = len(allp)
            last = None
            for i, (l, r) in enumerate(allp):
                last = e.matmul(out_ap, l, r, start=(i == 0), stop=(i == n - 1))
            return last

        return ph.op("tensor", fn, waits=waits, inc="pe")

    ph = new_ph("setup")
    td = []
    td.append(ph.dma("sync", id32[:, :], id32_d.ap()))
    td.append(ph.dma("sync", idbf[:, :], idbf_d.ap()))
    td.append(ph.dma("sync", onesf[:, :], ones_d.ap()))
    td.append(ph.dma("sync", onesb[:, :], onesbf_d.ap()))
    td.append(ph.dma("sync", ones16[:, :], ones16_d.ap()))
    td.append(ph.dma("sync", Acol[:, 0:1], apS(A_d, 0, [[1, NST], [1, 1]])))
    td.append(ph.dma("sync", bbcol[:, 0:1], apS(bb_d, 0, [[1, NST], [1, 1]])))
    td.append(ph.dma("sync", btcol[:, 0:1], apS(bt_d, 0, [[1, 1], [1, 1]])))
    tbph = ph.dma("sync", biascol[:, 0:1], bphi_d.ap(), sem="dinit")
    tbq = ph.dma("sync", bqdup[:, :], bq_d.ap(), sem="dinit")
    tbr = ph.dma("sync", brdup[:, :], br_d.ap(), sem="dinit")
    td.extend([tbph, tbq, tbr])
    for nm in brow:
        td.append(ph.dma("gpsimd", brow[nm][:, :],
                         apS(brow_d[nm], 0, [[D, 1], [1, D]])))
    td.append(ph.dma("sync", gwT[:, :],
                     apS(gwT_d, 0, [[E, 128], [128 * E, DT], [1, E]])))
    td.append(ph.dma("gpsimd", wbT[:, :],
                     apS(wbT_d, 0, [[NST, 128], [128 * NST, DT], [1, NST]])))
    td.append(ph.dma("gpsimd", wphi2[:, :], wphi_d.ap()))
    td.append(ph.dma("gpsimd", wcT[:, :], wcT_d.ap()))
    ph.op("vector", lambda e: e.memset(eps5[:, 0:1], 1e-5), inc="dve")
    ph.op("vector", lambda e: e.memset(onescol[:, 0:1], 1.0), inc="dve")
    tbadd = ph.op("vector",
                  lambda e: e.tensor_scalar(biascol[0:64, 0:1], biascol[0:64, 0:1],
                                            PI / 2, None, OP.add),
                  waits=[tbph, tbq, tbr], inc="dve")
    ph.op("vector", lambda e: e.tensor_scalar(threscol[:, 0:1], biascol[:, 0:1],
                                              -1.0, PI, OP.mult, OP.add),
          waits=[tbadd], inc="dve")
    tprev = None
    setup_ps = contextlib.ExitStack()
    ps_c = setup_ps.enter_context(nc.psum_tensor("ps_c", [128, DT], F32))
    trd = None
    for dd, dst in ((Dg_d, Dcol), (wtT_d, wtT)):
        tl = ph.dma("scalar", rows8[0:DT, :], dd.ap(), waits=[trd])
        tt = ph.op("tensor",
                   lambda e: e.transpose(ps_c[:, :], rows8[0:DT, :], id32[0:DT, 0:DT]),
                   waits=[tl, trd] + td, inc="pe")
        trd = ph.op("vector",
                    lambda e, dst=dst: e.tensor_copy(dst[:, :], ps_c[:, :]),
                    waits=[tt], inc="dve")
    setup_ps.close()

    def ln_phase(ph, pst, router, wait_tokens=()):
        xn32 = [lb(pst, "xn32", 128, D, F32) for i in range(2)]
        txn = [lb(pst, "txn", 128, D, F32) for i in range(2)] if router else None
        sqdump = lb(pst, "sqdump", 128, D, F32)
        psT = [pst.enter_context(nc.psum_tensor(f"psT{i}_{ph.name}", [128, D], F32))
               for i in range(2)]
        psr = ([pst.enter_context(nc.psum_tensor(f"psr{i}_{ph.name}", [128, 8], F32))
                for i in range(2)] if router else None)
        tsum = tsq = None
        for t in range(TT):
            tsum = ph.op("vector",
                         lambda e, t=t: e.reduce_sum(stats[:, t:t + 1], xsl(t), axis=AXX),
                         waits=list(wait_tokens),
                         inc="dve" if t == TT - 1 else None)
            tsq = ph.op("scalar",
                        lambda e, t=t: e.activation(
                            sqdump[:, :], xsl(t), AF.Square,
                            accum_out=stats[:, TT + t:TT + t + 1]),
                        waits=list(wait_tokens) + [tsq], inc="act")
        t1 = ph.op("vector",
                   lambda e: e.tensor_scalar_mul(stats[:, 2 * TT:3 * TT],
                                                 stats[:, 0:TT], 1.0 / D),
                   waits=[tsum], inc="dve")
        t2 = ph.op("vector",
                   lambda e: e.tensor_tensor(stats[:, 3 * TT:4 * TT],
                                             stats[:, 2 * TT:3 * TT],
                                             stats[:, 2 * TT:3 * TT], OP.mult),
                   waits=[t1], inc="dve")
        t3 = ph.op("vector",
                   lambda e: e.scalar_tensor_tensor(stats[:, 3 * TT:4 * TT],
                                                    stats[:, TT:2 * TT], 1.0 / D,
                                                    stats[:, 3 * TT:4 * TT],
                                                    OP.mult, OP.subtract),
                   waits=[t2, tsq], inc="dve")

        t4 = ph.op("scalar",
                   lambda e: e.activation(stats[:, 3 * TT:4 * TT],
                                          stats[:, 3 * TT:4 * TT],
                                          AF.Sqrt, bias=eps5[:, 0:1]),
                   waits=[t3], inc="act")
        t5 = ph.op("vector",
                   lambda e: e.reciprocal(stats[:, 3 * TT:4 * TT],
                                          stats[:, 3 * TT:4 * TT]),
                   waits=[t4], inc="dve")
        ev_toks = [None, None]
        tr_toks = [None, None]
        cp_toks = [None, None]
        rt_prev = None
        last_xn = None
        for t in range(TT):
            b = t % 2
            tts = ph.op("vector",
                        lambda e, t=t, b=b: e.tensor_scalar(
                            xn32[b][:, :], xsl(t),
                            stats[:, 2 * TT + t:2 * TT + t + 1],
                            stats[:, 3 * TT + t:3 * TT + t + 1],
                            OP.subtract, OP.mult),
                        waits=[t5, tr_toks[b]], inc="dve")

            def trf(e, b=b):
                last = None
                for j in range(DT):
                    last = e.transpose(psT[b][:, j * 128:(j + 1) * 128],
                                       xn32[b][:, j * 128:(j + 1) * 128],
                                       id32[:, :])
                return last

            ttr = ph.op("tensor", trf,
                        waits=[tts, ev_toks[b], cp_toks[b]], inc="pe")
            tr_toks[b] = ttr
            ev_toks[b] = ph.op("vector",
                               lambda e, t=t, b=b: e.tensor_copy(
                                   apS(xnT, t * 128,
                                       [[DT * S, 128], [S, DT], [1, 128]]),
                                   apS(psT[b], 0,
                                       [[D, 128], [128, DT], [1, 128]])),
                               waits=[ttr], inc="dve")
            last_xn = ev_toks[b]
            if router:
                cp_toks[b] = ph.op("scalar",
                                   lambda e, b=b: e.copy(txn[b][:, :], psT[b][:, :]),
                                   waits=[ttr, ev_toks[b]], inc="act")
                if ROUTER_LITE[0] >= 2:
                    rt_prev = ph.op("vector",
                                    lambda e, t=t: e.memset(comb[:, t * 8:t * 8 + E], 0.125),
                                    waits=[rt_prev], inc="dve")
                    continue
                trr = mmgroup(ph, psr[b][:, 0:E],
                              [(txn[b][:, j * 128:(j + 1) * 128],
                                gwT[:, j * E:(j + 1) * E]) for j in range(DT)],
                              waits=[cp_toks[b], rt_prev])
                c0 = t * 8
                if ROUTER_LITE[0] == 1:
                    rt_prev = ph.op("vector",
                                    lambda e, b=b, c0=c0: e.tensor_copy(
                                        comb[:, c0:c0 + E], psr[b][:, 0:E]),
                                    waits=[trr, rt_prev], inc="dve")
                    continue
                a1 = ph.op("vector",
                           lambda e, b=b: e.reduce_max(rsc[:, 0:1], psr[b][:, 0:E], axis=AXX),
                           waits=[trr, rt_prev], inc="dve")
                a2 = ph.op("vector",
                           lambda e, b=b: e.tensor_scalar(rsc[:, 12:12 + E], psr[b][:, 0:E],
                                                          rsc[:, 0:1], None, OP.is_ge),
                           waits=[a1], inc="dve")
                a3 = ph.op("vector",
                           lambda e, b=b: e.scalar_tensor_tensor(rsc[:, 4:4 + E],
                                                                 rsc[:, 12:12 + E], -1e30,
                                                                 psr[b][:, 0:E],
                                                                 OP.mult, OP.add),
                           waits=[a2], inc="dve")
                a4 = ph.op("vector",
                           lambda e: e.reduce_max(rsc[:, 1:2], rsc[:, 4:4 + E], axis=AXX),
                           waits=[a3], inc="dve")
                a5 = ph.op("vector",
                           lambda e: e.tensor_scalar(rsc[:, 20:20 + E], rsc[:, 4:4 + E],
                                                     rsc[:, 1:2], None, OP.is_ge),
                           waits=[a4], inc="dve")
                a6 = ph.op("vector",
                           lambda e: e.tensor_tensor(rsc[:, 2:3], rsc[:, 1:2],
                                                     rsc[:, 0:1], OP.subtract),
                           waits=[a5], inc="dve")
                a7 = ph.op("scalar",
                           lambda e: e.activation(rsc[:, 3:4], rsc[:, 2:3], AF.Exp),
                           waits=[a6], inc="act")
                a8 = ph.op("vector",
                           lambda e: e.tensor_scalar(rsc[:, 2:3], rsc[:, 3:4],
                                                     1.0, None, OP.add),
                           waits=[a7], inc="dve")
                a9 = ph.op("vector",
                           lambda e: e.reciprocal(rsc[:, 2:3], rsc[:, 2:3]),
                           waits=[a8], inc="dve")
                aA = ph.op("vector",
                           lambda e: e.tensor_tensor(rsc[:, 3:4], rsc[:, 3:4],
                                                     rsc[:, 2:3], OP.mult),
                           waits=[a9], inc="dve")
                aB = ph.op("vector",
                           lambda e: e.tensor_scalar(rsc[:, 28:28 + E], rsc[:, 20:20 + E],
                                                     rsc[:, 3:4], None, OP.mult),
                           waits=[aA], inc="dve")
                rt_prev = ph.op("vector",
                                lambda e, c0=c0: e.scalar_tensor_tensor(
                                    comb[:, c0:c0 + E], rsc[:, 12:12 + E],
                                    rsc[:, 2:3], rsc[:, 28:28 + E],
                                    OP.mult, OP.add),
                                waits=[aB], inc="dve")
        return last_xn, rt_prev

    def attn_a(ph, pst, wo_tok, loc):
        delt = loc["delt"]
        delt2 = loc["delt"]
        Aexp = loc["Aexp"]
        Bu = loc["Bu"]
        h32 = loc["h32"]
        h16 = loc["h16"]
        psA = pst.enter_context(nc.psum_tensor(f"psA_{ph.name}", [NST, S], F32))
        psB = psA
        tdel = []
        for c in range(TC):
            tdel.append(mmgroup(ph, psA[0:1, c * 512:(c + 1) * 512],
                                [(wtT[:, j:j + 1], xnsl(j, c * 512, 512))
                                 for j in range(DT)]))
        tsig = ph.op("scalar",
                     lambda e: e.activation(delt[:, :], psA[0:1, :], AF.Sigmoid,
                                            bias=btcol[0:1, 0:1]),
                     waits=tdel, inc="act")
        tdsc = ph.op("vector",
                     lambda e: e.tensor_scalar(delt2[:, :], delt[:, :], 0.99, 0.01,
                                               OP.mult, OP.add),
                     waits=[tsig], inc="dve")
        tbu = []
        for c in range(TC):
            tbu.append(mmgroup(ph, psB[:, c * 512:(c + 1) * 512],
                               [(wbT[:, j * NST:(j + 1) * NST], xnsl(j, c * 512, 512))
                                for j in range(DT)],
                               waits=[tsig, tdsc]))
        tbue = ph.op("scalar",
                     lambda e: e.activation(Bu[:, :], psB[:, :], AF.Identity,
                                            bias=bbcol[:, 0:1]),
                     waits=tbu + [tsig], inc="act")
        tbc = []
        for c in range(TC):
            tbc.append(mmgroup(ph, psA[:, c * 512:(c + 1) * 512],
                               [(ones16[0:1, 0:NST],
                                 delt2[0:1, c * 512:(c + 1) * 512])],
                               waits=[tsig, tdsc, tbue]))
        texp = ph.op("scalar",
                     lambda e: e.activation(Aexp[:, :], psA[:, :], AF.Exp,
                                            scale=Acol[:, 0:1]),
                     waits=tbc + [tbue], inc="act")
        tscan = ph.op("vector",
                      lambda e: e.tensor_tensor_scan(h32[:, :], Aexp[:, :], Bu[:, :],
                                                     0.0, OP.mult, OP.add),
                      waits=[texp, tbue], inc="dve")
        th16 = ph.op("vector", lambda e: e.tensor_copy(h16[:, :], h32[:, :]),
                     waits=[tscan], inc="dve")
        return th16, texp

    def attn_a2(ph, pst, th16, texp, wo_tok, loc):
        h16 = loc["h16"]
        py = [pst.enter_context(nc.psum_tensor(f"py{i}_{ph.name}", [128, 512], F32))
              for i in range(2)]
        po = [pst.enter_context(nc.psum_tensor(f"po{i}_{ph.name}", [128, 512], F32))
              for i in range(2)]
        ye = [None, None]
        last_ye = None
        idx = 0
        for j in range(DT):
            for c in range(TC):
                b = idx % 2
                tmm = mmgroup(ph, py[b][:, :],
                              [(wcT[:, j * 128:(j + 1) * 128],
                                h16[:, c * 512:(c + 1) * 512])],
                              waits=[th16, ye[b], texp],
                              extra=[(brow["bc"][0:1, j * 128:(j + 1) * 128],
                                      onesb[0:1, 0:512])])
                ye[b] = ph.op("vector",
                              lambda e, j=j, c=c, b=b: e.scalar_tensor_tensor(
                                  bigsl(j * S + c * 512, 512),
                                  xnsl(j, c * 512, 512), Dcol[:, j:j + 1],
                                  py[b][:, :], OP.mult, OP.add),
                              waits=[tmm], inc="dve")
                last_ye = ye[b]
                idx += 1
        xe = [None, None]
        idx = 0
        for t in range(TT):
            for c in range(HC):
                b = idx % 2
                tmm = mmgroup(ph, po[b][:, :],
                              [(bigsl(j * S + t * 128, 128),
                                wfull[:, j * D + c * 512: j * D + c * 512 + 512])
                               for j in range(DT)],
                              waits=[last_ye, xe[b], wo_tok],
                              extra=[(onesb[0:1, 0:128],
                                      brow["boa"][0:1, c * 512:(c + 1) * 512])])
                xe[b] = ph.op("vector",
                              lambda e, t=t, c=c, b=b: e.tensor_tensor(
                                  xsl(t, c * 512, 512), xsl(t, c * 512, 512),
                                  po[b][:, :], OP.add),
                              waits=[tmm], inc="dve")
                idx += 1

    def ffn(ph, pst, rt_tok):
        wg = [lb(pst, "wg", 128, DT * 512, BF16) for i in range(2)]
        wu = [lb(pst, "wu", 128, DT * 512, BF16) for i in range(2)]
        wd = [lb(pst, "wd", 128, 4 * D, BF16)]
        gtmp = [lb(pst, "gtmp", 128, 512, BF16) for i in range(2)]
        psG = [pst.enter_context(nc.psum_tensor(f"psG{i}_{ph.name}", [128, 512], F32))
               for i in range(2)]
        psU = [pst.enter_context(nc.psum_tensor(f"psU{i}_{ph.name}", [128, 512], F32))
               for i in range(2)]
        psD = [pst.enter_context(nc.psum_tensor(f"psD{i}_{ph.name}", [128, 512], F32))
               for i in range(2)]
        NWIN = E * NW
        wtoks = []
        consume = [None] * NWIN
        he = {}
        xe = [None, None]

        def issue_w(win):
            e_, w_ = divmod(win, NW)
            wb = win % 2
            wcons = consume[win - 2] if win >= 2 else None
            gsrc = apS(guT_d, e_ * D * 2 * F + w_ * 512,
                       [[2 * F, 128], [128 * 2 * F, DT], [1, 512]])
            usrc = apS(guT_d, e_ * D * 2 * F + F + w_ * 512,
                       [[2 * F, 128], [128 * 2 * F, DT], [1, 512]])
            sm = f"dgp{wb}"
            ph.dma("gpsimd", wg[wb][:, :], gsrc, waits=[wcons], sem=sm)
            t2 = ph.dma("gpsimd", wu[wb][:, :], usrc, waits=[wcons], sem=sm)
            return t2

        def issue_wd(win):
            e_, w_ = divmod(win, NW)
            dsrc = apS(dwT_d, e_ * F * D + w_ * 512 * D,
                       [[D, 128], [128 * D, 4], [1, D]])
            wcons = consume[win - 1] if win >= 1 else None
            return ph.dma("gpsimd", wd[0][:, :], dsrc, waits=[wcons], sem="dwd")

        for win in range(min(2, NWIN)):
            wtoks.append((issue_w(win), issue_wd(win) if win == 0 else None))
        gidx = 0
        didx = 0
        for e_ in range(E):
            for w_ in range(NW):
                win = e_ * NW + w_
                wb = win % 2
                wt, wdt = wtoks[win]
                if wdt is None:
                    wdt = issue_wd(win)
                last_hact = None
                for i in range(4):
                    for c in range(TC):
                        bg = gidx % 2
                        tg_ = mmgroup(ph, psG[bg][:, :],
                                      [(wg[wb][:, j * 512 + i * 128:
                                               j * 512 + i * 128 + 128],
                                        xnsl(j, c * 512, 512)) for j in range(DT)],
                                      waits=[wt, he.get(("g", gidx - 2))])
                        tu_ = mmgroup(ph, psU[bg][:, :],
                                      [(wu[wb][:, j * 512 + i * 128:
                                               j * 512 + i * 128 + 128],
                                        xnsl(j, c * 512, 512)) for j in range(DT)],
                                      waits=[wt, he.get(("h", gidx - 2))])
                        tge = ph.op("scalar",
                                    lambda e, bg=bg: e.activation(
                                        gtmp[bg][:, :], psG[bg][:, :], AF.Gelu),
                                    waits=[tg_, he.get(("h", gidx - 2))],
                                    inc="act")
                        he[("g", gidx)] = tge
                        th = ph.op("vector",
                                   lambda e, i=i, c=c, bg=bg: e.tensor_tensor(
                                       bigsl(i * S + c * 512, 512), psU[bg][:, :],
                                       gtmp[bg][:, :], OP.mult),
                                   waits=[tu_, tge,
                                          consume[win - 1] if win >= 1 else None],
                                   inc="dve")
                        he[("h", gidx)] = th
                        last_hact = th
                        gidx += 1
                last_mm = None
                for t in range(TT):
                    for hcc in range(HC):
                        bd = didx % 2
                        tdn = mmgroup(ph, psD[bd][:, :],
                                      [(bigsl(i * S + t * 128, 128),
                                        wd[0][:, i * D + hcc * 512:
                                              i * D + hcc * 512 + 512])
                                       for i in range(4)],
                                      waits=[last_hact, wdt, xe[bd]])
                        xe[bd] = ph.op("vector",
                                       lambda e, t=t, hcc=hcc, bd=bd, e_=e_:
                                       e.scalar_tensor_tensor(
                                           xsl(t, hcc * 512, 512), psD[bd][:, :],
                                           comb[:, t * 8 + e_:t * 8 + e_ + 1],
                                           xsl(t, hcc * 512, 512),
                                           OP.mult, OP.add),
                                       waits=[tdn, rt_tok], inc="dve")
                        last_mm = tdn
                        didx += 1
                consume[win] = last_mm
                nxt = win + 2
                if nxt < NWIN:
                    wtoks.append((issue_w(nxt), None))

    def attnb_z(ph, pst, last_xn, wz_tok):
        xn32 = [lb(pst, "zn32", 128, D, F32) for i in range(2)]
        txn = [lb(pst, "ztxn", 128, D, F32) for i in range(2)]
        znb = [lb(pst, "znb", 128, D, BF16) for i in range(2)]
        sqdump = lb(pst, "zsqdump", 128, D, F32)
        gbc = lb(pst, "gbc", 128, D, F32)
        bbc = lb(pst, "bbc", 128, D, F32)
        browf = {nm: lb(pst, "zrow_" + nm, 1, D, F32) for nm in ("lng", "lnb")}
        psZ = [pst.enter_context(nc.psum_tensor(f"psZ{i}_{ph.name}", [128, D], F32))
               for i in range(2)]
        psbz = pst.enter_context(nc.psum_tensor(f"psbz_{ph.name}", [128, 512], F32))
        tgb = None
        trow = {nm: ph.dma("sync", browf[nm][:, :],
                           apS(brow_d[nm], 0, [[D, 1], [1, D]]))
                for nm in ("lng", "lnb")}
        for nm, dst in (("lng", gbc), ("lnb", bbc)):
            for c in range(HC):
                tmm = mmgroup(ph, psbz[:, :],
                              [(onesf[0:1, 0:128],
                                browf[nm][0:1, c * 512:(c + 1) * 512])],
                              waits=[tgb, trow["lnb"]])
                tgb = ph.op("vector",
                            lambda e, dst=dst, c=c: e.tensor_copy(
                                dst[:, c * 512:(c + 1) * 512], psbz[:, :]),
                            waits=[tmm], inc="dve")
        rd = [[], []]
        tsq = None
        tsum = None
        for t in range(TT):
            b = t % 2
            grp = []
            for hc in range(HC):
                grp.append(mmgroup(ph, psZ[b][:, hc * 512:hc * 512 + 512],
                                   [(xnsl(j, t * 128, 128),
                                     wfull[:, j * D + hc * 512: j * D + hc * 512 + 512])
                                    for j in range(DT)],
                                   waits=[last_xn, wz_tok] + rd[b],
                                   extra=[(onesb[0:1, 0:128],
                                           brow["bz"][0:1, hc * 512:hc * 512 + 512])]))
            tsum = ph.op("vector",
                         lambda e, t=t, b=b: e.reduce_sum(zstats[:, t:t + 1],
                                                          psZ[b][:, :], axis=AXX),
                         waits=grp, inc="dve")
            tcp = ph.op("vector",
                        lambda e, t=t, b=b: e.tensor_copy(bigsl(t * D, D),
                                                          psZ[b][:, :]),
                        waits=grp, inc="dve")
            tsq = ph.op("scalar",
                        lambda e, t=t, b=b: e.activation(
                            sqdump[:, :], psZ[b][:, :], AF.Square,
                            accum_out=zstats[:, TT + t:TT + t + 1]),
                        waits=grp + [tsq, tcp], inc="act")
            rd[b] = [tsum, tsq, tcp]
        t1 = ph.op("vector",
                   lambda e: e.tensor_scalar_mul(zstats[:, 2 * TT:3 * TT],
                                                 zstats[:, 0:TT], 1.0 / D),
                   waits=[tsum], inc="dve")
        t2 = ph.op("vector",
                   lambda e: e.tensor_tensor(zstats[:, 3 * TT:4 * TT],
                                             zstats[:, 2 * TT:3 * TT],
                                             zstats[:, 2 * TT:3 * TT], OP.mult),
                   waits=[t1], inc="dve")
        t3 = ph.op("vector",
                   lambda e: e.scalar_tensor_tensor(zstats[:, 3 * TT:4 * TT],
                                                    zstats[:, TT:2 * TT], 1.0 / D,
                                                    zstats[:, 3 * TT:4 * TT],
                                                    OP.mult, OP.subtract),
                   waits=[t2, tsq], inc="dve")
        t4 = ph.op("scalar",
                   lambda e: e.activation(zstats[:, 3 * TT:4 * TT],
                                          zstats[:, 3 * TT:4 * TT],
                                          AF.Sqrt, bias=eps5[:, 0:1]),
                   waits=[t3], inc="act")
        t5 = ph.op("vector",
                   lambda e: e.reciprocal(zstats[:, 3 * TT:4 * TT],
                                          zstats[:, 3 * TT:4 * TT]),
                   waits=[t4], inc="dve")
        psT = [pst.enter_context(nc.psum_tensor(f"psTz{i}_{ph.name}", [128, D], BF16))
               for i in range(2)]
        tr_toks = [None, None]
        ev_toks = [None, None]
        tz3p = [None, None]
        last_zn = None
        for t in range(TT):
            b = t % 2
            tz1 = ph.op("vector",
                        lambda e, t=t, b=b: e.tensor_scalar(
                            xn32[b][:, :], bigsl(t * D, D),
                            zstats[:, 2 * TT + t:2 * TT + t + 1],
                            zstats[:, 3 * TT + t:3 * TT + t + 1],
                            OP.subtract, OP.mult),
                        waits=[t5, tr_toks[b]], inc="dve")
            tz2 = ph.op("vector",
                        lambda e, b=b: e.tensor_tensor(txn[b][:, :], xn32[b][:, :],
                                                       gbc[:, :], OP.mult),
                        waits=[tz1, tz3p[b]], inc="dve")
            tz3 = ph.op("vector",
                        lambda e, b=b: e.tensor_tensor(znb[b][:, :], txn[b][:, :],
                                                       bbc[:, :], OP.add),
                        waits=[tz2, tr_toks[b]], inc="dve")
            tz3p[b] = tz3

            def trf(e, b=b):
                last = None
                for j in range(DT):
                    last = e.transpose(psT[b][:, j * 128:(j + 1) * 128],
                                       znb[b][:, j * 128:(j + 1) * 128],
                                       idbf[:, :])
                return last

            ttr = ph.op("tensor", trf, waits=[tz3, ev_toks[b]], inc="pe")
            tr_toks[b] = ttr
            ev_toks[b] = ph.op("vector",
                               lambda e, t=t, b=b: e.tensor_copy(
                                   apS(xnT, t * 128,
                                       [[DT * S, 128], [S, DT], [1, 128]]),
                                   apS(psT[b], 0,
                                       [[D, 128], [128, DT], [1, 128]])),
                               waits=[ttr], inc="dve")
            last_zn = ev_toks[b]
        return last_zn

    def attnb_qrs(ph, pst, last_zn):
        wqs = [lb(pst, "wqs", 128, DT * 128, BF16)]
        wrs = [lb(pst, "wrs", 128, DT * 128, BF16)]
        whs = [lb(pst, "whs", 128, DT * 128, BF16)]
        Qc = [lb(pst, "Qc", 128, S, BF16) for i in range(2)]
        Rc = [lb(pst, "Rc", 128, S, BF16) for i in range(2)]
        RcN = lb(pst, "RcN", 128, S, BF16)
        hv = [lb(pst, "hv", 128, S, BF16) for i in range(2)]
        mrg = lb(pst, "mrg", 128, 5248, BF16)
        scb = [mrg[:, 0:512], mrg[:, 512:1024]]
        tmid = [mrg[:, 1024 + i * 512:1536 + i * 512] for i in range(4)]
        tsin = [mrg[:, 3072 + i * 512:3584 + i * 512] for i in range(4)]
        Gsb = mrg[:, 5120:5248]
        prod = mrg[:, 0:512]
        nrmrow = mrg[0:1, 512:1024]
        rcols = lb(pst, "rcols", 128, TT, F32)
        psA = [pst.enter_context(nc.psum_tensor(f"psA{i}_{ph.name}", [128, 512], F32))
               for i in range(2)]
        psS = [pst.enter_context(nc.psum_tensor(f"psS{i}_{ph.name}", [128, 512], F32))
               for i in range(2)]
        psO = pst.enter_context(nc.psum_tensor(f"psO_{ph.name}", [128, TT * 64], F32))
        psNG = pst.enter_context(nc.psum_tensor(f"psNG_{ph.name}", [128, 512], F32))
        psTb = pst.enter_context(nc.psum_tensor(f"psTb_{ph.name}", [128, 512], BF16))

        NH = D // 64
        psA_ev = [None, None]
        wq_cons = [None]
        wr_cons = [None]
        wh_cons = [None]
        wq_tok = [None] * NH
        wr_tok = [None] * NH
        wh_tok = [None] * (NH // 2)

        qcrc_free = [None, None]
        hv_free = [None, None]
        whs_used = [None]
        psTb_ev = [None]
        psNG_free = [None]
        gsb_used = [None]
        prod_used = [None]
        nrm_used = [None]
        rcols_used = [None]
        oev_last = [None]
        tmid_rd = [None] * 4
        tsin_rd = [None] * 4
        trig_last = [None, None]

        def issue_qr(h):
            tq = tr = None
            for half in range(2):
                tq = ph.dma("gpsimd",
                            apS(wqs[0], half * 64, [[DT * 128, 128], [128, DT], [1, 64]]),
                            apS(wq_d, h * 64, [[D, 128], [128 * D, DT], [1, 64]]),
                            waits=[wq_cons[0]], sem="dgp0")
                tr = ph.dma("gpsimd",
                            apS(wrs[0], half * 64, [[DT * 128, 128], [128, DT], [1, 64]]),
                            apS(wr_d, h * 64, [[D, 128], [128 * D, DT], [1, 64]]),
                            waits=[wr_cons[0]], sem="dgp0")
            wq_tok[h] = tr
            wr_tok[h] = tr

        def issue_h(hp):
            wh_tok[hp] = ph.dma("gpsimd", whs[0][:, :],
                                apS(wh_d, hp * 128, [[D, 128], [128 * D, DT], [1, 128]]),
                                waits=[whs_used[0]], sem="dgp1")

        def stage1(h):
            b = h % 2
            if h % 2 == 0:
                hp = h // 2
                hb = hp % 2
                for t4 in range(TT // 4):
                    par = t4 % 2

                    def hvf(e, t4=t4, hb=hb, hp=hp, par=par):
                        last = None
                        for tl in range(4):
                            t = t4 * 4 + tl
                            for j in range(DT):
                                last = e.matmul(psA[par][:, tl * 128:(tl + 1) * 128],
                                                xnsl(j, t * 128, 128),
                                                whs[0][:, j * 128:(j + 1) * 128],
                                                start=(j == 0), stop=False)
                            last = e.matmul(psA[par][:, tl * 128:(tl + 1) * 128],
                                            onesb[0:1, 0:128],
                                            brow["bh"][0:1, hp * 128:hp * 128 + 128],
                                            start=False, stop=True)
                        return last

                    tmm = ph.op("tensor", hvf,
                                waits=[wh_tok[hp], last_zn, psA_ev[par], hv_free[hb]],
                                inc="pe")
                    whs_used[0] = tmm
                    psA_ev[par] = ph.op("vector",
                                        lambda e, t4=t4, hb=hb, par=par: e.tensor_copy(
                                            hv[hb][:, t4 * 512:(t4 + 1) * 512],
                                            psA[par][:, :]),
                                        waits=[tmm], inc="dve")
                if hp + 1 < NH // 2:
                    issue_h(hp + 1)
            phis = []
            for (wsb, wtk, bcol, dstC, is_q) in ((wqs, wq_tok[h], bqdup, Qc[b], True),
                                                 (wrs, wr_tok[h], brdup, Rc[b], False)):
                evs = []
                for c in range(TC):
                    par = c % 2
                    tmm = mmgroup(ph, psA[par][:, :],
                                  [(wsb[0][:, j * 128:(j + 1) * 128],
                                    xnsl(j, c * 512, 512)) for j in range(DT)],
                                  waits=[wtk, last_zn, psA_ev[par]])
                    if is_q:
                        wq_cons[0] = tmm
                    else:
                        wr_cons[0] = tmm
                    ev = ph.op("scalar",
                               lambda e, c=c, par=par, bcol=bcol, h=h, dstC=dstC:
                               e.activation(
                                   dstC[:, c * 512:(c + 1) * 512], psA[par][:, :],
                                   AF.Identity, bias=bcol[:, h:h + 1]),
                               waits=[tmm, qcrc_free[b]], inc="act")
                    psA_ev[par] = ev
                    evs.append(ev)
                for c in range(TC):
                    phis.append((evs[c], c, dstC, is_q))
            t5s_q = t5s_r = None
            for idx, (evc, c, dstC, is_q) in enumerate(phis):
                p = idx % 4
                par = idx % 2
                tmm = mmgroup(ph, psA[par][:, :],
                              [(wphi2[:, :], dstC[0:64, c * 512:(c + 1) * 512])],
                              waits=[evc, psA_ev[par]])
                t2 = ph.op("vector",
                           lambda e, p=p, par=par: e.tensor_scalar(
                               tmid[p], psA[par][:, :], threscol[:, 0:1], None,
                               OP.is_gt),
                           waits=[tmm, tmid_rd[p]], inc="dve")
                t3 = ph.op("vector",
                           lambda e, p=p, par=par: e.scalar_tensor_tensor(
                               tmid[p], tmid[p], -2.0 * PI,
                               psA[par][:, :], OP.mult, OP.add),
                           waits=[t2], inc="dve")
                psA_ev[par] = t3
                t4 = ph.op("scalar",
                           lambda e, p=p: e.activation(
                               tsin[p], tmid[p], AF.Sin, bias=biascol[:, 0:1]),
                           waits=[t3, tsin_rd[p]], inc="act")
                tmid_rd[p] = t4
                t5 = ph.op("vector",
                           lambda e, p=p, c=c, dstC=dstC: e.tensor_tensor(
                               dstC[:, c * 512:(c + 1) * 512],
                               dstC[:, c * 512:(c + 1) * 512],
                               tsin[p], OP.mult),
                           waits=[t4], inc="dve")
                tsin_rd[p] = t5
                if is_q:
                    t5s_q = t5
                else:
                    t5s_r = t5
            trig_last[0] = t5s_q
            trig_last[1] = t5s_r
            if h + 1 < NH:
                issue_qr(h + 1)
            return trig_last[0], trig_last[1]

        def stage3(h, tq_done, tr_done):
            b = h % 2
            tr_evs = []
            for g in range(TKT // 4):

                def trf(e, g=g, b=b):
                    last = None
                    for i in range(4):
                        kt = g * 4 + i
                        last = e.transpose(psTb[:, i * 128:(i + 1) * 128],
                                           Rc[b][:, kt * 128:(kt + 1) * 128],
                                           idbf[:, :])
                    return last

                ttr = ph.op("tensor", trf, waits=[tr_done, psTb_ev[0]], inc="pe")
                tev = ph.op("vector",
                            lambda e, g=g: e.tensor_copy(
                                RcN[:, g * 512:(g + 1) * 512], psTb[:, :]),
                            waits=[ttr], inc="dve")
                psTb_ev[0] = tev
                tr_evs.append(tev)

            def gf(e):
                last = None
                for kt in range(TKT):
                    last = e.matmul(psNG[:, 0:128],
                                    RcN[:, kt * 128:(kt + 1) * 128],
                                    RcN[:, kt * 128:(kt + 1) * 128],
                                    start=(kt == 0), stop=(kt == TKT - 1))
                return last

            tg = ph.op("tensor", gf, waits=tr_evs + [psNG_free[0]], inc="pe")
            tgev = ph.op("vector", lambda e: e.tensor_copy(Gsb, psNG[:, 0:128]),
                         waits=[tg, gsb_used[0]], inc="dve")
            tn_prev = None
            ttp = None
            for qc in range(TC):
                par = qc % 2
                tq_ = mmgroup(ph, psA[par][:, :],
                              [(Gsb, Qc[b][:, qc * 512:(qc + 1) * 512])],
                              waits=[tgev, tq_done, psA_ev[par]])
                gsb_used[0] = tq_
                tp = ph.op("vector",
                           lambda e, par=par, qc=qc, b=b: e.tensor_tensor(
                               prod, psA[par][:, :],
                               Qc[b][:, qc * 512:(qc + 1) * 512], OP.mult),
                           waits=[tq_, prod_used[0]], inc="dve")
                psA_ev[par] = tp

                def nf(e):
                    return e.matmul(psNG[0:1, 0:512], onescol[:, 0:1], prod,
                                    start=True, stop=True)

                tn = ph.op("tensor", nf, waits=[tp, tgev, tn_prev], inc="pe")
                prod_used[0] = tn
                tc_ = ph.op("vector",
                            lambda e: e.tensor_copy(nrmrow, psNG[0:1, 0:512]),
                            waits=[tn, nrm_used[0]], inc="dve")
                tn_prev = tc_
                tf1 = ph.op("scalar", lambda e: e.activation(nrmrow, nrmrow, AF.Sqrt),
                            waits=[tc_], inc="act")
                tf2 = ph.op("vector", lambda e: e.tensor_scalar(nrmrow, nrmrow,
                                                                1e-12, None, OP.max),
                            waits=[tf1], inc="dve")

                def trp(e, qc=qc):
                    last = None
                    for i in range(4):
                        c0 = (qc * 4 + i) * 2
                        last = e.transpose(psTb[:, c0:c0 + 1],
                                           nrmrow[0:1, i * 128:(i + 1) * 128],
                                           idbf[0:1, 0:1])
                    return last

                ttp = ph.op("tensor", trp, waits=[tf2, psTb_ev[0]], inc="pe")
                nrm_used[0] = ttp
            trc0 = ph.op("vector", lambda e: e.tensor_copy(
                             rcols[:, 0:TT], apS(psTb, 0, [[512, 128], [2, TT]])),
                         waits=[ttp, rcols_used[0]], inc="dve")
            psTb_ev[0] = trc0
            trc = ph.op("vector", lambda e: e.reciprocal(rcols[:, 0:TT],
                                                         rcols[:, 0:TT]),
                        waits=[trc0], inc="dve")
            psNG_free[0] = tn_prev
            hb = (h // 2) % 2
            sc_cp = [None, None]
            prev_out = [None, None]
            last_out = None
            kidx = 0
            for qc in range(TC):
                for kt in range(TKT):
                    par = kidx % 2
                    tmm = mmgroup(ph, psS[par][:, :],
                                  [(Rc[b][:, kt * 128:(kt + 1) * 128],
                                    Qc[b][:, qc * 512:(qc + 1) * 512])],
                                  waits=[tq_done, tr_done, sc_cp[par]])
                    if kidx % 2 == 0:
                        tcp = ph.op("vector",
                                    lambda e, par=par: e.tensor_copy(
                                        scb[par], psS[par][:, :]),
                                    waits=[tmm, prev_out[par]], inc="dve")
                    else:
                        tcp = ph.op("scalar",
                                    lambda e, par=par: e.activation(
                                        scb[par], psS[par][:, :], AF.Identity),
                                    waits=[tmm, prev_out[par]], inc="act")
                    sc_cp[par] = tcp

                    def outf(e, kt=kt, qc=qc, par=par, h=h, hb=hb):
                        last = None
                        for qtl in range(4):
                            qt = qc * 4 + qtl
                            stt_ = (kt == 0 and qt % 8 == 0)
                            stp_ = (kt == TKT - 1 and (qt % 8 == 7 or qt == TT - 1))
                            last = e.matmul(
                                psO[:, qt * 64:(qt + 1) * 64],
                                mrg[:, par * 512 + qtl * 128:
                                    par * 512 + (qtl + 1) * 128],
                                hv[hb][:, kt * 128 + (h % 2) * 64:
                                       kt * 128 + (h % 2) * 64 + 64],
                                start=stt_, stop=stp_)
                        return last

                    tout = ph.op("tensor", outf, waits=[tcp, oev_last[0]], inc="pe")
                    prev_out[par] = tout
                    last_out = tout
                    kidx += 1
            qcrc_free[b] = last_out
            hv_free[hb] = last_out
            oev = None
            for qt in range(TT):
                oev = ph.op("scalar",
                            lambda e, qt=qt, h=h: e.activation(
                                bigsl(qt * D + h * 64, 64),
                                psO[:, qt * 64:(qt + 1) * 64], AF.Identity,
                                scale=rcols[:, qt:qt + 1]),
                            waits=[trc, last_out, last_zn], inc="act")
            oev_last[0] = oev
            rcols_used[0] = oev

        issue_qr(0)
        issue_h(0)
        pend = None
        for h in range(NH):
            tqd, trd = stage1(h)
            if pend is not None:
                stage3(*pend)
            pend = (h, tqd, trd)
        stage3(*pend)
        return [oev_last[0]]


    def attnb_wo(ph, pst, ot_toks, wfull_prev):
        wo_tok = ph.dma("gpsimd", wfull[:, :],
                        apS(wob_d, 0, [[D, 128], [128 * D, DT], [1, D]]),
                        waits=[wfull_prev[0]], sem="dwf")
        wfull_prev[0] = wo_tok
        psT = [pst.enter_context(nc.psum_tensor(f"psTo{i}_{ph.name}", [128, D], BF16))
               for i in range(2)]
        psF = [pst.enter_context(nc.psum_tensor(f"psF{i}_{ph.name}", [128, 512], F32))
               for i in range(2)]
        ev = [None, None]
        last_ev = None
        for qt in range(TT):
            b = qt % 2

            def trf(e, qt=qt, b=b):
                last = None
                for j in range(DT):
                    last = e.transpose(psT[b][:, j * 128:(j + 1) * 128],
                                       bigsl(qt * D + j * 128, 128), idbf[:, :])
                return last

            ttr = ph.op("tensor", trf, waits=ot_toks + [ev[b]], inc="pe")
            ev[b] = ph.op("vector",
                          lambda e, qt=qt, b=b: e.tensor_copy(
                              apS(xnT, qt * 128, [[DT * S, 128], [S, DT], [1, 128]]),
                              apS(psT[b], 0, [[D, 128], [128, DT], [1, 128]])),
                          waits=[ttr], inc="dve")
            last_ev = ev[b]
        xe = [None, None]
        idx = 0
        for t in range(TT):
            for c in range(HC):
                b = idx % 2
                tmm = mmgroup(ph, psF[b][:, :],
                              [(xnsl(j, t * 128, 128),
                                wfull[:, j * D + c * 512: j * D + c * 512 + 512])
                               for j in range(DT)],
                              waits=[last_ev, xe[b], wo_tok],
                              extra=[(onesb[0:1, 0:128],
                                      brow["bob"][0:1, c * 512:(c + 1) * 512])])
                xe[b] = ph.op("vector",
                              lambda e, t=t, c=c, b=b: e.tensor_tensor(
                                  xsl(t, c * 512, 512), xsl(t, c * 512, 512),
                                  psF[b][:, :], OP.add),
                              waits=[tmm], inc="dve")
                idx += 1

    store_toks = []
    prev_store = None
    wfull_prev = [None]
    for batch in range(NB):
        ph = new_ph(f"b{batch}_load")
        src = apS(x_d, batch * S * D, [[D, 128], [128 * D, TT], [1, D]])
        dst = apS(x_sb, 0, [[TT * D, 128], [D, TT], [1, D]])
        xl = ph.dma("sync", dst, src, waits=[prev_store])

        ph = new_ph(f"b{batch}_ln1")
        with contextlib.ExitStack() as pst:
            woa_tok = ph.dma("gpsimd", wfull[:, :],
                             apS(woTa_d, 0, [[D, 128], [128 * D, DT], [1, D]]),
                             waits=[wfull_prev[0]], sem="dwf")
            wfull_prev[0] = woa_tok
            ln_phase(ph, pst, router=False, wait_tokens=[xl])

        if "attn_a" in enable:
            ph = new_ph(f"b{batch}_attna")
            with contextlib.ExitStack() as loc_st:
                loc = {
                    "delt": lb(loc_st, "delt", 1, S, F32),
                    "Aexp": lb(loc_st, "Aexp", NST, S, F32),
                    "Bu": lb(loc_st, "Bu", NST, S, F32),
                    "h32": lb(loc_st, "h32", NST, S, F32),
                    "h16": lb(loc_st, "h16", NST, S, BF16),
                }
                with contextlib.ExitStack() as pst:
                    th16, texp = attn_a(ph, pst, woa_tok, loc)
                with contextlib.ExitStack() as pst:
                    attn_a2(ph, pst, th16, texp, woa_tok, loc)

        if "ffn1" in enable or "router" in enable:
            ph = new_ph(f"b{batch}_ln2")
            with contextlib.ExitStack() as pst:
                _, rt = ln_phase(ph, pst, router=True)
            if "ffn1" in enable:
                ph = new_ph(f"b{batch}_ffn1")
                with contextlib.ExitStack() as pst:
                    ffn(ph, pst, rt)

        if "attn_b" in enable:
            ph = new_ph(f"b{batch}_ln3")
            with contextlib.ExitStack() as pst:
                wz_tok = ph.dma("gpsimd", wfull[:, :],
                                apS(wz_d, 0, [[D, 128], [128 * D, DT], [1, D]]),
                                waits=[wfull_prev[0]], sem="dwf")
                wfull_prev[0] = wz_tok
                last_xn, _ = ln_phase(ph, pst, router=False)
            ph = new_ph(f"b{batch}_z")
            with contextlib.ExitStack() as pst:
                last_zn = attnb_z(ph, pst, last_xn, wz_tok)
            ph = new_ph(f"b{batch}_qrs")
            with contextlib.ExitStack() as pst:
                ot_toks = attnb_qrs(ph, pst, last_zn)
            ph = new_ph(f"b{batch}_wo")
            with contextlib.ExitStack() as pst:
                attnb_wo(ph, pst, ot_toks, wfull_prev)

        if "ffn2" in enable:
            ph = new_ph(f"b{batch}_ln4")
            with contextlib.ExitStack() as pst:
                _, rt = ln_phase(ph, pst, router=True)
            ph = new_ph(f"b{batch}_ffn2")
            with contextlib.ExitStack() as pst:
                ffn(ph, pst, rt)

        ph = new_ph(f"b{batch}_store")
        dst = apS(out_d, batch * S * D, [[D, 128], [128 * D, TT], [1, D]])
        src = apS(x_sb, 0, [[TT * D, 128], [D, TT], [1, D]])
        prev_store = ph.dma("sync", dst, src)
        store_toks.append(prev_store)

    ph = new_ph("final")
    ph.op("sync", lambda e: e.nop(), waits=store_toks)

    for ph_ in phases:
        ph_.emit()
    k.close()
    st.close()
    nc.finalize()
    return nc


def host_prep(cfg, inputs):
    DT = cfg["D"] // 128
    f32 = np.float32
    bf = ml_dtypes.bfloat16
    g = lambda n: np.asarray(inputs[n], dtype=f32)
    shared = {
        "guT": np.ascontiguousarray(g("L0_ffn_gate_up_w").transpose(0, 2, 1)),
        "dwT": np.ascontiguousarray(g("L0_ffn_down_w").transpose(0, 2, 1)),
        "gwT": np.ascontiguousarray(g("L0_ffn_gate_w").T),
        "woTa": np.ascontiguousarray(g("L0_attn_a_W_o").T),
        "wbT": np.ascontiguousarray(g("L0_attn_a_W_b").T),
        "wtT": np.ascontiguousarray(g("L0_attn_a_W_t").reshape(-1, 128)),
        "wcT": np.ascontiguousarray(g("L0_attn_a_W_c").T),
        "A": g("L0_attn_a_A"),
        "bb": g("L0_attn_a_b_b"),
        "bt": g("L0_attn_a_b_t"),
        "b_bc": g("L0_attn_a_b_c"),
        "b_boa": g("L0_attn_a_b_o"),
        "Dg": np.ascontiguousarray(g("L0_attn_a_D").reshape(DT, 128)),
        "wz": g("L0_attn_b_Wz"),
        "wq": g("L0_attn_b_Wq"),
        "wr": g("L0_attn_b_Wr"),
        "wh": g("L0_attn_b_Wh"),
        "wob": g("L0_attn_b_Wo"),
        "b_bz": g("L0_attn_b_bz"),
        "bq": np.ascontiguousarray(np.vstack([g("L0_attn_b_bq").reshape(16, 64).T] * 2)),
        "br": np.ascontiguousarray(np.vstack([g("L0_attn_b_br").reshape(16, 64).T] * 2)),
        "b_bh": g("L0_attn_b_bh"),
        "b_bob": g("L0_attn_b_bo"),
        "b_lng": g("L0_attn_b_ln_g"),
        "b_lnb": g("L0_attn_b_ln_b"),
        "wphi": np.ascontiguousarray(np.concatenate([g("L0_attn_b_Wphi")] * 2, axis=1)),
        "bphi": np.ascontiguousarray(np.concatenate([g("L0_attn_b_bphi")] * 2).reshape(128, 1)),
        "id32": np.eye(128, dtype=f32),
        "idbf": np.eye(128, dtype=bf),
        "ones_f": np.ones((1, 128), dtype=f32),
        "ones_bf": np.ones((1, 512), dtype=bf),
        "ones16": np.ones((1, 16), dtype=f32),
    }
    return shared


def kernel(**inputs):
    cfg = CFG_FULL
    NB = cfg["BATCHES"]
    x = np.asarray(inputs["L0_attn_a_x"], dtype=np.float32)
    n_cores = 8
    shared = host_prep(cfg, inputs)
    nc = build(cfg)
    in_maps = []
    for c in range(n_cores):
        m = dict(shared)
        m["x"] = np.ascontiguousarray(x[c * NB:(c + 1) * NB])
        in_maps.append(m)
    res = run_bass_kernel_spmd(nc, in_maps, core_ids=list(range(n_cores)))
    outs = [np.asarray(r["out"]) for r in res.results]
    return np.concatenate(outs, axis=0).astype(np.float32)
